# version 17
# baseline (speedup 1.0000x reference)
import contextlib
import numpy as np

import concourse.bass as bass
import concourse.mybir as mybir
from concourse.bass_utils import run_bass_kernel_spmd

F32 = mybir.dt.float32
BF16 = mybir.dt.bfloat16
AF = mybir.ActivationFunctionType
ALU = mybir.AluOpType
AX = mybir.AxisListType.X

D = 1024
E = 6
FC = 8
NT = 512
SEQ = 4
TLOC = 128
NCHUNK_FULL = 16
RING = 7
LN_EPS = 1e-5
RMS_EPS = 1e-6

SLOT_SUP = 36
SLOT_SG = 40
SLOT_SD = 42
SLOT_WZ = 44
SLOT_WH = 46
NSLOT = 48


def _halves(W):
    W = np.asarray(W, dtype=np.float32)
    K, M = W.shape
    assert K == 1024 and M % 512 == 0
    out = []
    for c in range(0, M, 512):
        H = W[:, c:c + 512].reshape(8, 128, 512).transpose(1, 0, 2).reshape(128, 4096)
        out.append(np.ascontiguousarray(H))
    return out


def _pcol(v):
    return np.ascontiguousarray(np.asarray(v, np.float32).reshape(8, 128).T)


def host_inputs(ins, core_id):
    x = np.ascontiguousarray(ins["L0_attn_h_prev_layer"][4 * core_id:4 * core_id + 4])

    halves = []
    for e in range(E):
        halves += _halves(ins["L0_ffn_exp_Wup"][e])
        halves += _halves(ins["L0_ffn_exp_Wg"][e])
        halves += _halves(ins["L0_ffn_exp_Wdown"][e])
    halves += _halves(ins["L0_ffn_share_Wup"])
    halves += _halves(ins["L0_ffn_share_Wg"])
    halves += _halves(ins["L0_ffn_share_Wdown"])
    halves += _halves(np.asarray(ins["L0_attn_W_z"], np.float32).T)
    halves += _halves(np.asarray(ins["L0_attn_W_h"], np.float32).T)
    wblob = np.stack(halves)

    bup = np.asarray(ins["L0_ffn_exp_bup"], np.float32).reshape(E, 8, 128)
    bup = np.ascontiguousarray(bup.transpose(2, 0, 1).reshape(128, E * 8))
    bg = np.asarray(ins["L0_ffn_exp_bg"], np.float32).reshape(E, 8, 128)
    bg = np.ascontiguousarray(bg.transpose(2, 0, 1).reshape(128, E * 8))
    nw = np.asarray(ins["L0_ffn_norm_w"], np.float32).reshape(2, 8, 128)
    nw = np.ascontiguousarray(nw.transpose(2, 0, 1).reshape(128, 16))
    rw = np.asarray(ins["L0_ffn_router_W"], np.float32).reshape(2, 8, 128, E)
    rw = np.ascontiguousarray(rw.transpose(2, 0, 1, 3).reshape(128, 2 * 8 * E))

    return {
        "x_in": x,
        "wblob": wblob,
        "t_bz": _pcol(ins["L0_attn_b_z"]),
        "t_bh": _pcol(ins["L0_attn_b_h"]),
        "t_bup": bup,
        "t_bg": bg,
        "t_bsup": np.ascontiguousarray(
            np.asarray(ins["L0_ffn_share_bup"], np.float32).reshape(16, 128).T),
        "t_bsg": _pcol(ins["L0_ffn_share_bg"]),
        "t_bsd": _pcol(ins["L0_ffn_share_bdown"]),
        "t_nw": nw,
        "t_rw": rw,
        "t_rb": np.ascontiguousarray(np.asarray(ins["L0_ffn_router_b"], np.float32).T),
        "t_gb": np.ascontiguousarray(np.asarray(ins["L0_ffn_gate_bias"], np.float32).T),
        "t_bdn": np.ascontiguousarray(np.asarray(ins["L0_ffn_exp_bdown"], np.float32)),
        "t_id": np.eye(128, dtype=np.float32),
        "t_oh": np.ascontiguousarray(
            np.repeat(np.eye(6, dtype=np.float32), 128, axis=1)),
    }


class Sem:
    def __init__(self, h):
        self.h = h
        self.n = 0


class Builder:
    ENGS = ("sync", "tensor", "vector", "scalar", "gpsimd")

    def __init__(self):
        self.ops = {e: [] for e in self.ENGS}
        self.last_wait = {e: {} for e in self.ENGS}

    def em(self, eng, fn, waits=(), incs=()):
        lw = self.last_wait[eng]
        ws = []
        for s, v in waits:
            if v <= 0:
                continue
            if lw.get(id(s), -1) >= v:
                continue
            lw[id(s)] = v
            ws.append((s, v))
        merged = {}
        for s, a in incs:
            merged[id(s)] = (s, merged.get(id(s), (s, 0))[1] + a)
        incs = list(merged.values())
        assert len(incs) <= 1, f"multiple sems inc'd by one op: {incs}"
        for s, a in incs:
            s.n += a
        self.ops[eng].append((tuple(ws), fn, tuple(incs)))

    def replay(self, block):
        for eng in self.ENGS:
            dec = getattr(block, eng)
            ops = self.ops[eng]

            def make(ops):
                def _body(engobj):
                    for ws, fn, incs in ops:
                        for s, v in ws:
                            engobj.wait_ge(s.h, v)
                        ins = fn(engobj)
                        for s, a in incs:
                            ins = ins.then_inc(s.h, a)
                return _body
            dec(make(ops))


def cap(sem):
    return (sem, sem.n)


class Banks:

    def __init__(self, tiles):
        self.t = tiles
        self.state = [[] for _ in tiles]
        self.rr = 0

    def acq(self):
        n = len(self.t)
        for k in range(n):
            i = (self.rr + k) % n
            if self.state[i] != "inuse":
                edges = list(self.state[i])
                self.state[i] = "inuse"
                self.rr = i + 1
                return i, edges
        raise RuntimeError("all PSUM banks in use")

    def release(self, i, edge):
        if self.state[i] == "inuse":
            self.state[i] = [edge]
        else:
            self.state[i].append(edge)


def build_nc(n_chunks=NCHUNK_FULL):
    nc = bass.Bass("TRN2", target_bir_lowering=False, debug=False)

    x_in = nc.dram_tensor("x_in", [SEQ, 2048, D], F32, kind="ExternalInput").ap()
    wblob = nc.dram_tensor("wblob", [NSLOT, 128, 4096], F32, kind="ExternalInput").ap()
    names_small = ["t_bz", "t_bh", "t_bup", "t_bg", "t_bsup", "t_bsg", "t_bsd",
                   "t_nw", "t_rw", "t_rb", "t_gb", "t_bdn", "t_id", "t_oh"]
    shapes_small = {"t_bz": [128, 8], "t_bh": [128, 8], "t_bup": [128, 48],
                    "t_bg": [128, 48], "t_bsup": [128, 16], "t_bsg": [128, 8],
                    "t_bsd": [128, 8], "t_nw": [128, 16], "t_rw": [128, 96],
                    "t_rb": [6, 2], "t_gb": [6, 2], "t_bdn": [6, 1024],
                    "t_id": [128, 128], "t_oh": [6, 768]}
    dsmall = {n: nc.dram_tensor(n, shapes_small[n], F32, kind="ExternalInput").ap()
              for n in names_small}
    y_out = nc.dram_tensor("y_out", [SEQ, 2048, D], F32, kind="ExternalOutput").ap()
    wscr = nc.dram_tensor("wscr", [NSLOT, 128, 4096], BF16, kind="Internal").ap()

    ctx = contextlib.ExitStack()
    with ctx:
        def sb(name, shape, dt):
            return ctx.enter_context(nc.sbuf_tensor(name, shape, dt))

        wring = sb("wring", [128, RING * 4096], BF16)
        xA = sb("xA", [128, 4096], F32)
        xB = sb("xB", [128, 4096], F32)
        xf = sb("xf", [128, 4096], F32)
        xnb = sb("xnb", [128, 4096], BF16)
        hb2 = sb("hb2", [128, 8192], BF16)
        yac = sb("yac", [128, 4096], F32)
        prod = sb("prod", [128, 4096], BF16)
        sgt = sb("sgt", [128, 1024], F32)
        p1t = sb("p1t", [128, 1024], F32)
        tnt = sb("tnt", [128, 1024], F32)
        stage = sb("stage", [128, 5120], F32)
        rowA = sb("rowA", [1, 512], F32)
        rowB = sb("rowB", [1, 512], F32)
        Rs = sb("Rs", [128, 512], F32)
        MRs = sb("MRs", [128, 512], F32)
        rt_b6 = sb("rt_b6", [6, 512], F32)
        rt_g6 = sb("rt_g6", [6, 512], F32)
        w6b = sb("w6b", [6, 512], BF16)
        rt_bt = sb("rt_bt", [128, 24], F32)
        rt_gt = sb("rt_gt", [128, 24], F32)
        r_eq = sb("r_eq", [128, 24], F32)
        r_b2 = sb("r_b2", [128, 24], F32)
        r_sel = sb("r_sel", [128, 24], F32)
        r_gs = sb("r_gs", [128, 24], F32)
        r_d = sb("r_d", [128, 24], F32)
        r_wp = sb("r_wp", [128, 24], F32)
        r_w = sb("r_w", [128, 24], F32)
        r_m1 = sb("r_m1", [128, 4], F32)
        r_m2 = sb("r_m2", [128, 4], F32)
        r_S = sb("r_S", [128, 4], F32)
        carry = sb("carry", [128, 32], F32)
        bz = sb("bz", [128, 8], F32)
        nbz = sb("nbz", [128, 8], F32)
        bh = sb("bh", [128, 8], F32)
        bup = sb("bup", [128, 48], F32)
        bgt_ = sb("bgt", [128, 48], F32)
        bsup = sb("bsup", [128, 16], F32)
        bsg = sb("bsg", [128, 8], F32)
        bsd = sb("bsd", [128, 8], F32)
        csd = sb("csd", [128, 8], F32)
        nw = sb("nw", [128, 16], F32)
        rw = sb("rw", [128, 96], F32)
        rb = sb("rb", [6, 2], F32)
        gb = sb("gb", [6, 2], F32)
        bdn = sb("bdn", [6, 1024], BF16)
        oh6 = sb("oh6", [6, 768], BF16)
        idt = sb("idt", [128, 128], F32)
        c_eps_ln = sb("c_eps_ln", [1, 1], F32)
        c_eps_rms = sb("c_eps_rms", [1, 1], F32)
        nc.const_aps.register(F32, LN_EPS, c_eps_ln.ap())
        nc.const_aps.register(F32, RMS_EPS, c_eps_rms.ap())
        nwrow8 = sb("nwrow8", [1, 1024], F32)
        ones_bc = sb("ones_bc", [128, 1], BF16)
        ones_fr = sb("ones_fr", [1, 128], F32)

        bank_t = [ctx.enter_context(nc.psum_tensor(f"pb{i}", [128, 512], F32))
                  for i in range(8)]

        S = {n: Sem(ctx.enter_context(nc.semaphore(n))) for n in
             ("BL", "XL", "OG", "T", "V", "SS", "G")}
        BL, XL, OG = S["BL"], S["XL"], S["OG"]
        T, V, SS, G = S["T"], S["V"], S["SS"], S["G"]
        WLp = [Sem(ctx.enter_context(nc.semaphore(f"WL{i}"))) for i in range(RING)]
        CVg = [Sem(ctx.enter_context(nc.semaphore(f"CV{i}"))) for i in range(6)]
        CV_GROUPS = [2, 4, 6, 12, 12, 12]
        CV_OF = []
        for gi, n_ in enumerate(CV_GROUPS):
            CV_OF += [gi] * n_

        B = Builder()
        banks = Banks(bank_t)

        def bankap(i, sl=None):
            return bank_t[i][:, :] if sl is None else bank_t[i][sl[0], sl[1]]

        sb_map = {"t_bz": bz, "t_bh": bh, "t_bup": bup, "t_bg": bgt_,
                  "t_bsup": bsup, "t_bsg": bsg, "t_bsd": bsd, "t_nw": nw,
                  "t_rw": rw, "t_rb": rb, "t_gb": gb, "t_id": idt}
        for n, tile in sb_map.items():
            B.em("gpsimd", lambda g, t=tile, a=dsmall[n]: g.dma_start(out=t[:, :], in_=a),
                 incs=[(BL, 16)])
        B.em("gpsimd", lambda g: g.dma_start(out=bdn[:, :], in_=dsmall["t_bdn"]),
             incs=[(BL, 16)])
        B.em("gpsimd", lambda g: g.dma_start(out=oh6[:, :], in_=dsmall["t_oh"]),
             incs=[(BL, 16)])
        bl_total = BL.n

        for s_ in range(NSLOT):
            B.em("gpsimd", lambda g, s=s_: g.dma_start(out=wscr[s], in_=wblob[s]),
                 incs=[(CVg[CV_OF[s_]], 16)])
        conv_done = [(CVg[CV_OF[s]], 16 * CV_GROUPS[CV_OF[s]]) for s in range(NSLOT)]

        B.em("vector", lambda v: v.memset(ones_bc[:, :], 1.0), waits=[cap(BL)])
        B.em("vector", lambda v: v.memset(c_eps_ln[:, :], LN_EPS))
        B.em("vector", lambda v: v.memset(c_eps_rms[:, :], RMS_EPS))
        B.em("vector", lambda v: v.memset(ones_fr[:, :], 1.0))
        B.em("vector", lambda v: v.memset(carry[:, :], 0.0))
        B.em("vector", lambda v: v.tensor_scalar_mul(nbz[:, :], bz[:, :], -1.0))
        B.em("vector", lambda v: v.tensor_scalar_mul(csd[:, :], bsd[:, :], 2.0),
             incs=[(V, 1)])
        pre_v = V.n

        B.em("tensor", lambda t: t.nop(), waits=[(BL, bl_total), (V, pre_v)])
        B.em("scalar", lambda s: s.nop(), waits=[(BL, bl_total), (V, pre_v)])
        B.em("gpsimd", lambda g: g.nop(), waits=[(BL, bl_total)])

        class Ring:
            def __init__(self):
                self.load_i = 0
                self.use_i = 0
                self.load_wl = []
                self.free_wf = []

            def pump(self, order):
                while (self.load_i < min(len(order), self.use_i + RING)
                       and (self.load_i < RING
                            or self.free_wf[self.load_i - RING] is not None)):
                    li = self.load_i
                    slot = order[li]
                    pos = li % RING
                    waits = []
                    if li >= RING:
                        waits.append((T, self.free_wf[li - RING]))
                    waits.append(conv_done[slot])
                    B.em("sync",
                         lambda s, p=pos, sl=slot: s.dma_start(
                             out=wring[:, p * 4096:(p + 1) * 4096], in_=wscr[sl]),
                         waits=waits, incs=[(WLp[pos], 16)])
                    self.load_wl.append((WLp[pos], WLp[pos].n))
                    self.free_wf.append(None)
                    self.load_i += 1

            def use(self, order, expect_slot):
                i = self.use_i
                assert order[i] == expect_slot, (i, order[i], expect_slot)
                self.pump(order)
                self.use_i += 1
                pos = i % RING
                return i, pos, self.load_wl[i]

            def freed(self, i, order):
                self.free_wf[i] = T.n
                self.pump(order)

        ring = Ring()

        def blk_up(k):
            if k < E:
                return [6 * k, 6 * k + 1]
            return ([SLOT_SUP, SLOT_SUP + 1] if k == E
                    else [SLOT_SUP + 2, SLOT_SUP + 3])

        def blk_gate(k):
            return [6 * k + 2, 6 * k + 3] if k < E else [SLOT_SG, SLOT_SG + 1]

        def blk_down(k):
            return [6 * k + 4, 6 * k + 5] if k < E else [SLOT_SD, SLOT_SD + 1]

        it_order = blk_up(0) + blk_up(1)
        for k in range(8):
            if k != 7:
                it_order += blk_gate(k)
            if k + 2 < 8:
                it_order += blk_up(k + 2)
            if k != 7:
                it_order += blk_down(k)
        attn_order = [SLOT_WZ, SLOT_WH, SLOT_WZ + 1, SLOT_WH + 1]
        chunk_order = it_order * 2 + attn_order + it_order * 2
        full_order = chunk_order * n_chunks

        def wslice(pos, kc, t):
            return wring[:, pos * 4096 + kc * 512 + t * 128:
                         pos * 4096 + kc * 512 + t * 128 + 128]

        def v3(ap, g):
            return ap.rearrange("p (g e) -> p g e", g=g)

        def vo1(ap):
            return ap.rearrange("p (g o) -> p g o", o=1)

        qg = {"V0": 0, "V1": 0}

        def stats_and_rows(x_sig, is_ln, xbuf, eps):
            qview = hb2[:, 0:4096]
            sq_sig = []
            cast_sig = []
            for fc in range(FC):
                xs = x_sig[fc] if isinstance(x_sig[fc], list) else [x_sig[fc]]
                B.em("scalar", lambda s, f=fc, xb=xbuf: s.activation(
                    qview[:, f * 512:(f + 1) * 512], xb[:, f * 512:(f + 1) * 512],
                    AF.Square, bias=0.0, scale=1.0),
                    waits=xs + [cap(T), cap(G), (V, qg["V0"])], incs=[(SS, 1)])
                sq_sig.append((SS, SS.n))
                if is_ln:
                    B.em("scalar", lambda s, f=fc, xb=xbuf: s.activation(
                        xnb[:, f * 512:(f + 1) * 512], xb[:, f * 512:(f + 1) * 512],
                        AF.Identity, bias=0.0, scale=1.0),
                        waits=xs + [cap(T)], incs=[(SS, 1)])
                    cast_sig.append((SS, SS.n))
            bsq, bsq_e = banks.acq()
            for kc in range(FC):
                B.em("tensor", lambda t, b=bsq, k=kc: t.matmul(
                    bankap(b, (slice(0, 1), slice(0, 512))), ones_bc[:, :],
                    qview[:, k * 512:(k + 1) * 512], start=(k == 0), stop=(k == 7)),
                    waits=(bsq_e if kc == 0 else []) + [sq_sig[kc]],
                    incs=([(T, 1)] if kc == 7 else []))
            if is_ln:
                bsum, bsum_e = banks.acq()
                for kc in range(FC):
                    B.em("tensor", lambda t, b=bsum, k=kc: t.matmul(
                        bankap(b, (slice(0, 1), slice(0, 512))), ones_bc[:, :],
                        xnb[:, k * 512:(k + 1) * 512], start=(k == 0), stop=(k == 7)),
                        waits=(bsum_e if kc == 0 else []) + [cast_sig[kc]],
                        incs=([(T, 1)] if kc == 7 else []))
                B.em("vector", lambda v, b=bsum: v.tensor_scalar_mul(
                    rowA[:, :], bankap(b, (slice(0, 1), slice(0, 512))), 1.0 / D),
                    waits=[cap(T)], incs=[(V, 1)])
                B.em("vector", lambda v: v.tensor_mul(
                    rowB[:, :], rowA[:, :], rowA[:, :]),
                    waits=[(V, V.n)], incs=[(V, 1)])
                B.em("vector", lambda v, b=bsq: v.scalar_tensor_tensor(
                    rowB[:, :], bankap(b, (slice(0, 1), slice(0, 512))), 1.0 / D,
                    rowB[:, :], ALU.mult, ALU.subtract),
                    waits=[(V, V.n)], incs=[(V, 1)])
                banks.release(bsum, cap(V))
                banks.release(bsq, cap(V))
                B.em("scalar", lambda s, e=eps: s.activation(
                    rowB[:, :], rowB[:, :], AF.Sqrt, bias=e, scale=1.0),
                    waits=[cap(V)], incs=[(SS, 1)])
                B.em("vector", lambda v: v.reciprocal(rowB[:, :], rowB[:, :]),
                     waits=[cap(SS)], incs=[(V, 1)])
                rstd_sig = (V, V.n)
                B.em("vector", lambda v: v.tensor_mul(
                    rowA[:, :], rowA[:, :], rowB[:, :]),
                    waits=[(V, V.n)], incs=[(V, 1)])
                rowA_sig = (V, V.n)
            else:
                B.em("scalar", lambda s, b=bsq, e=eps: s.activation(
                    rowB[:, :], bankap(b, (slice(0, 1), slice(0, 512))), AF.Sqrt,
                    bias=e, scale=1.0 / D),
                    waits=[cap(T)], incs=[(SS, 1)])
                banks.release(bsq, cap(SS))
                B.em("vector", lambda v: v.reciprocal(rowB[:, :], rowB[:, :]),
                     waits=[cap(SS)], incs=[(V, 1)])
                rstd_sig = (V, V.n)
                rowA_sig = None
            return rstd_sig, rowA_sig

        def bcast_to_sbuf(row_tile, dst, src_sig):
            bi, be = banks.acq()
            B.em("tensor", lambda t, b=bi, r=row_tile: t.matmul(
                bankap(b), ones_fr[:, :], r[:, :], start=True, stop=True),
                waits=be + [src_sig], incs=[(T, 1)])
            B.em("vector", lambda v, b=bi, d=dst: v.tensor_copy(d[:, :], bankap(b)),
                 waits=[cap(T), cap(G)], incs=[(V, 1)])
            banks.release(bi, cap(V))

        def layernorm(xbuf, x_sig, eps):
            rstd_sig, rowA_sig = stats_and_rows(x_sig, True, xbuf, eps)
            bcast_to_sbuf(rowB, Rs, rstd_sig)
            bcast_to_sbuf(rowA, MRs, rowA_sig)
            r_v = V.n
            xf_sig, xnb_sig = [], []
            for fc in range(FC):
                B.em("vector", lambda v, f=fc, xb=xbuf: v.tensor_tensor(
                    tnt[:, (f % 2) * 512:(f % 2 + 1) * 512],
                    xb[:, f * 512:(f + 1) * 512],
                    Rs[:, :], ALU.mult),
                    waits=[(V, r_v if fc < 2 else xf_sig[fc - 2][1]), cap(G)],
                    incs=[(V, 1)])
                tv = V.n
                B.em("vector", lambda v, f=fc: v.tensor_sub(
                    xf[:, f * 512:(f + 1) * 512],
                    tnt[:, (f % 2) * 512:(f % 2 + 1) * 512],
                    MRs[:, :]),
                    waits=[(V, tv)], incs=[(V, 1)])
                xf_sig.append((V, V.n))
                B.em("scalar", lambda s, f=fc: s.activation(
                    xnb[:, f * 512:(f + 1) * 512], xf[:, f * 512:(f + 1) * 512],
                    AF.Identity, bias=0.0, scale=1.0),
                    waits=[(V, V.n)], incs=[(SS, 1)])
                xnb_sig.append((SS, SS.n))
            return xf_sig, xnb_sig

        def rms(j, yac_sig):
            rstd_sig, _ = stats_and_rows(yac_sig, False, yac, RMS_EPS)
            bcast_to_sbuf(rowB, Rs, rstd_sig)
            r_v = V.n
            xf_sig, xnb_sig = [], []
            for fc in range(FC):
                B.em("vector", lambda v, f=fc, jj=j: v.tensor_single_scalar(
                    tnt[:, (f % 2) * 512:(f % 2 + 1) * 512],
                    yac[:, f * 512:(f + 1) * 512],
                    nw[:, jj * 8 + f:jj * 8 + f + 1], ALU.mult),
                    waits=[(V, r_v), cap(G)], incs=[(V, 1)])
                tv = V.n
                B.em("gpsimd", lambda g, f=fc: g.tensor_mul(
                    xf[:, f * 512:(f + 1) * 512],
                    tnt[:, (f % 2) * 512:(f % 2 + 1) * 512],
                    Rs[:, :]),
                    waits=[(V, tv)], incs=[(G, 1)])
                xf_sig.append((G, G.n))
                B.em("scalar", lambda s, f=fc: s.activation(
                    xnb[:, f * 512:(f + 1) * 512], xf[:, f * 512:(f + 1) * 512],
                    AF.Identity, bias=0.0, scale=1.0),
                    waits=[(G, G.n)], incs=[(SS, 1)])
                xnb_sig.append((SS, SS.n))
            return xf_sig, xnb_sig

        def router_mm(j, xf_sig):
            bi, be = banks.acq()
            for kc in range(FC):
                B.em("tensor", lambda t, b=bi, k=kc, jj=j: t.matmul(
                    bankap(b, (slice(0, 6), slice(0, 512))),
                    rw[:, jj * 48 + k * 6: jj * 48 + (k + 1) * 6],
                    xf[:, k * 512:(k + 1) * 512], start=(k == 0), stop=(k == 7)),
                    waits=(be if kc == 0 else []) + [xf_sig[kc]],
                    incs=([(T, 1)] if kc == 7 else []))
            B.em("vector", lambda v, b=bi, jj=j: v.tensor_scalar(
                rt_b6[:, :], bankap(b, (slice(0, 6), slice(0, 512))),
                rb[:, jj:jj + 1], gb[:, jj:jj + 1], ALU.add, ALU.add),
                waits=[cap(T)])
            B.em("vector", lambda v, b=bi, jj=j: v.tensor_single_scalar(
                rt_g6[:, :], bankap(b, (slice(0, 6), slice(0, 512))),
                rb[:, jj:jj + 1], ALU.add), incs=[(V, 1)])
            banks.release(bi, cap(V))
            return V.n

        def router_transposes(rt_v):
            bt1, bt1e = banks.acq()
            bt2, bt2e = banks.acq()
            for grp in range(4):
                B.em("tensor", lambda t, b=bt1, g_=grp: t.transpose(
                    bankap(b, (slice(0, 128), slice(g_ * 6, (g_ + 1) * 6))),
                    rt_b6[:, g_ * 128:(g_ + 1) * 128], idt[0:6, 0:6]),
                    waits=(bt1e if grp == 0 else []) + [(V, rt_v)])
            for grp in range(4):
                B.em("tensor", lambda t, b=bt2, g_=grp: t.transpose(
                    bankap(b, (slice(0, 128), slice(g_ * 6, (g_ + 1) * 6))),
                    rt_g6[:, g_ * 128:(g_ + 1) * 128], idt[0:6, 0:6]),
                    waits=(bt2e if grp == 0 else []),
                    incs=([(T, 1)] if grp == 3 else []))
            B.em("vector", lambda v, b=bt1: v.tensor_copy(
                rt_bt[:, :], bankap(b, (slice(0, 128), slice(0, 24)))),
                waits=[cap(T)])
            B.em("vector", lambda v, b=bt2: v.tensor_copy(
                rt_gt[:, :], bankap(b, (slice(0, 128), slice(0, 24)))),
                incs=[(V, 1)])
            banks.release(bt1, cap(V))
            banks.release(bt2, cap(V))
            B.em("vector", lambda v: v.drain())
            B.em("vector", lambda v: v.reduce_max(vo1(r_m1.ap()), v3(rt_bt.ap(), 4), AX))
            B.em("vector", lambda v: v.drain())
            for g_ in range(4):
                B.em("vector", lambda v, gg=g_: v.tensor_single_scalar(
                    r_eq[:, gg * 6:(gg + 1) * 6], rt_bt[:, gg * 6:(gg + 1) * 6],
                    r_m1[:, gg:gg + 1], ALU.is_ge))
            B.em("vector", lambda v: v.drain())
            for g_ in range(4):
                B.em("vector", lambda v, gg=g_: v.scalar_tensor_tensor(
                    r_b2[:, gg * 6:(gg + 1) * 6], r_eq[:, gg * 6:(gg + 1) * 6],
                    -30000.0, rt_bt[:, gg * 6:(gg + 1) * 6], ALU.mult, ALU.add))
            B.em("vector", lambda v: v.drain())
            B.em("vector", lambda v: v.reduce_max(vo1(r_m2.ap()), v3(r_b2.ap(), 4), AX))
            B.em("vector", lambda v: v.drain())
            for g_ in range(4):
                B.em("vector", lambda v, gg=g_: v.tensor_single_scalar(
                    r_sel[:, gg * 6:(gg + 1) * 6], rt_bt[:, gg * 6:(gg + 1) * 6],
                    r_m2[:, gg:gg + 1], ALU.is_ge))
            B.em("vector", lambda v: v.drain())
            B.em("vector", lambda v: v.tensor_mul(r_gs[:, :], rt_gt[:, :], r_sel[:, :]))
            B.em("vector", lambda v: v.drain())
            B.em("vector", lambda v: v.reduce_sum(vo1(r_S.ap()), v3(r_gs.ap(), 4), AX))
            B.em("vector", lambda v: v.drain())
            for g_ in range(4):
                B.em("vector", lambda v, gg=g_: v.tensor_scalar(
                    r_d[:, gg * 6:(gg + 1) * 6], rt_gt[:, gg * 6:(gg + 1) * 6],
                    2.0, r_S[:, gg:gg + 1], ALU.mult, ALU.subtract),
                    incs=([(V, 1)] if g_ == 3 else []))
            B.em("scalar", lambda s: s.activation(
                r_wp[:, :], r_d[:, :], AF.Sigmoid, bias=0.0, scale=1.0),
                waits=[cap(V)], incs=[(SS, 1)])
            B.em("vector", lambda v: v.tensor_mul(r_w[:, :], r_wp[:, :], r_sel[:, :]),
                 waits=[cap(SS)], incs=[(V, 1)])
            return V.n

        def router_wtrans(w_v):
            bw, bwe = banks.acq()
            for g_ in range(4):
                B.em("tensor", lambda t, b=bw, gg=g_: t.transpose(
                    bankap(b, (slice(0, 6), slice(gg * 128, (gg + 1) * 128))),
                    r_w[:, gg * 6:(gg + 1) * 6], idt[:, :]),
                    waits=(bwe if g_ == 0 else []) + [(V, w_v)],
                    incs=([(T, 1)] if g_ == 3 else []))
            B.em("vector", lambda v, b=bw: v.tensor_copy(
                w6b[:, :], bankap(b, (slice(0, 6), slice(0, 512)))),
                waits=[cap(T)], incs=[(V, 1)])
            banks.release(bw, cap(V))
            return V.n

        def phase_up(use0, use1, bias_tile, bias_off, xnb_sig, par):
            uses = [use0, use1]
            hoff = par * 4096
            for oc in range(FC):
                half = oc // 4
                i, pos, wl = uses[half]
                bi, be = banks.acq()
                for kc in range(FC):
                    incs = [(T, 1)] if kc == 7 else []
                    B.em("tensor", lambda t, b=bi, k=kc, p=pos, tt=oc % 4: t.matmul(
                        bankap(b), wslice(p, k, tt),
                        xnb[:, k * 512:(k + 1) * 512], start=(k == 0), stop=(k == 7)),
                        waits=(be if kc == 0 else []) +
                              ([wl] if kc == 0 else []) + [xnb_sig[kc]],
                        incs=incs)
                if oc % 4 == 3:
                    ring.freed(i, full_order)
                B.em("scalar", lambda s, b=bi, o=oc, bo=bias_off + oc, bt=bias_tile:
                     s.activation(hb2[:, hoff + o * 512: hoff + o * 512 + 512],
                                  bankap(b), AF.Identity,
                                  bias=bt[:, bo:bo + 1], scale=1.0),
                     waits=[cap(T), cap(G), (V, qg["V%d" % par])], incs=[(SS, 1)])
                banks.release(bi, cap(SS))
            return SS.n

        def phase_gate_prod(useg0, useg1, bias_off, wrow_bank, shared, h_ss, par,
                            free_ring=True):
            uses = [useg0, useg1]
            hoff = par * 4096
            for oc in range(FC):
                half = oc // 4
                i, pos, wl = uses[half]
                bi, be = banks.acq()
                for kc in range(FC):
                    incs = [(T, 1)] if kc == 7 else []
                    B.em("tensor", lambda t, b=bi, k=kc, p=pos, tt=oc % 4: t.matmul(
                        bankap(b), wslice(p, k, tt),
                        hb2[:, hoff + k * 512: hoff + (k + 1) * 512],
                        start=(k == 0), stop=(k == 7)),
                        waits=(be if kc == 0 else []) + ([wl] if kc == 0 else []) +
                              ([(SS, h_ss)] if kc == 0 else []),
                        incs=incs)
                if oc % 4 == 3 and free_ring:
                    ring.freed(i, full_order)
                sl = oc % 2
                B.em("scalar", lambda s, b=bi, ss=sl, bo=bias_off + oc,
                     bt=(bsg if shared else bgt_):
                     s.activation(sgt[:, ss * 512:(ss + 1) * 512], bankap(b), AF.Silu,
                                  bias=bt[:, bo:bo + 1], scale=1.0),
                     waits=[cap(T), cap(G)] + ([cap(V)] if shared else []),
                     incs=[(SS, 1)])
                banks.release(bi, cap(SS))
                if shared:
                    B.em("vector", lambda v, ss=sl, o=oc: v.tensor_tensor(
                        prod[:, o * 512:(o + 1) * 512],
                        sgt[:, ss * 512:(ss + 1) * 512],
                        hb2[:, hoff + o * 512: hoff + (o + 1) * 512], ALU.mult),
                        waits=[cap(SS), cap(T)], incs=[(V, 1)])
                else:
                    B.em("gpsimd", lambda g, ss=sl, o=oc: g.tensor_tensor(
                        p1t[:, ss * 512:(ss + 1) * 512],
                        sgt[:, ss * 512:(ss + 1) * 512],
                        hb2[:, hoff + o * 512: hoff + (o + 1) * 512], ALU.mult),
                        waits=[cap(SS), cap(V)], incs=[(G, 1)])
                    wb = bankap(wrow_bank).rearrange("p (o t) -> p o t", o=1)
                    B.em("vector", lambda v, ss=sl, o=oc, w=wb: v.tensor_tensor(
                        prod.ap()[:, o * 512:(o + 1) * 512].rearrange(
                            "p (o t) -> p o t", o=1),
                        p1t.ap()[:, ss * 512:(ss + 1) * 512].rearrange(
                            "p (o t) -> p o t", o=1),
                        w, ALU.mult),
                        waits=[cap(G), cap(T)], incs=[(V, 1)])
            if not shared:
                banks.release(wrow_bank, cap(V))
            return V.n

        def phase_down(used0, used1, first_slot, with_bdn, w6b_v, prod_v, xf_sig,
                       yac_sig_out, free_ring=True):
            uses = [used0, used1]
            B.em("vector", lambda v: v.drain())
            for m in range(FC):
                half = m // 4
                i, pos, wl = uses[half]
                bi, be = banks.acq()
                for kc in range(FC):
                    incs = [(T, 1)] if (kc == 7 and not with_bdn) else []
                    B.em("tensor", lambda t, b=bi, k=kc, p=pos, tt=m % 4: t.matmul(
                        bankap(b), wslice(p, k, tt),
                        prod[:, k * 512:(k + 1) * 512],
                        start=(k == 0), stop=(k == 7 and not with_bdn)),
                        waits=(be if kc == 0 else []) + ([wl] if kc == 0 else []) +
                              ([(V, prod_v)] if kc == 0 else []),
                        incs=incs)
                if with_bdn:
                    B.em("tensor", lambda t, b=bi, mm=m: t.matmul(
                        bankap(b), bdn[:, mm * 128:(mm + 1) * 128],
                        w6b[:, :], start=False, stop=True),
                        waits=[(V, w6b_v)], incs=[(T, 1)])
                if m % 4 == 3 and free_ring:
                    ring.freed(i, full_order)
                if first_slot:
                    B.em("vector", lambda v, b=bi, mm=m: v.scalar_tensor_tensor(
                        yac[:, mm * 512:(mm + 1) * 512], bankap(b),
                        csd[:, mm:mm + 1], xf[:, mm * 512:(mm + 1) * 512],
                        ALU.add, ALU.add),
                        waits=[cap(T), xf_sig[m]], incs=[(V, 1)])
                else:
                    B.em("vector", lambda v, b=bi, mm=m: v.tensor_tensor(
                        yac[:, mm * 512:(mm + 1) * 512], yac[:, mm * 512:(mm + 1) * 512],
                        bankap(b), ALU.add),
                        waits=[cap(T)], incs=[(V, 1)])
                if yac_sig_out is not None:
                    yac_sig_out[m] = (V, V.n)
                banks.release(bi, cap(V))

        def ffn_iter(j, xf_sig, xnb_sig, fillers=None):
            rt_v = router_mm(j, xf_sig)
            u0 = [ring.use(full_order, s) for s in blk_up(0)]
            h_ss = {0: phase_up(u0[0], u0[1], bup, 0, xnb_sig, 0)}
            w_v = router_transposes(rt_v)
            u1 = [ring.use(full_order, s) for s in blk_up(1)]
            h_ss[1] = phase_up(u1[0], u1[1], bup, 8, xnb_sig, 1)
            w6b_v = router_wtrans(w_v)
            yac_sig = [None] * FC
            for k in range(8):
                if fillers and k in (1, 3, 5, 7):
                    fillers.pop(0)()
                shared = k >= E
                if not shared:
                    bw, bwe = banks.acq()
                    B.em("tensor", lambda t, b=bw, ee=k: t.matmul(
                        bankap(b), oh6[:, ee * 128:(ee + 1) * 128], w6b[0:6, :],
                        start=True, stop=True),
                        waits=bwe + [(V, w6b_v)], incs=[(T, 1)])
                else:
                    bw = None
                if k == 7:
                    g_ = hold_sg
                else:
                    g_ = [ring.use(full_order, s) for s in blk_gate(k)]
                if k == 6:
                    hold_sg = g_
                prod_v = phase_gate_prod(
                    g_[0], g_[1], (k * 8 if not shared else 0), bw, shared,
                    h_ss[k], k % 2, free_ring=(k != 6))
                if shared:
                    qg["V%d" % (k % 2)] = V.n
                if k + 2 < 8:
                    nu = [ring.use(full_order, s) for s in blk_up(k + 2)]
                if k == 7:
                    d_ = hold_sd
                else:
                    d_ = [ring.use(full_order, s) for s in blk_down(k)]
                if k == 6:
                    hold_sd = d_
                if k + 2 < 8:
                    kk = k + 2
                    bias_t, bias_o = (bup, kk * 8) if kk < E else (bsup, (kk - E) * 8)
                    h_ss[kk] = phase_up(nu[0], nu[1], bias_t, bias_o, xnb_sig, kk % 2)
                phase_down(d_[0], d_[1], k == 0, k == 7, w6b_v, prod_v,
                           xf_sig, yac_sig, free_ring=(k != 6))
            return rms(j, yac_sig)

        def at(par, idx):
            off = par * 2048 + idx * 512
            return yac.ap()[:, off:off + 512]

        atrk = {"scan_v": [0, 0], "b_g": [0, 0], "carry_ss": [0] * FC}

        def attn(xsrc, xdst, xnb_sig):
            uz = [ring.use(full_order, SLOT_WZ)]
            uh = [ring.use(full_order, SLOT_WH)]
            uz.append(ring.use(full_order, SLOT_WZ + 1))
            uh.append(ring.use(full_order, SLOT_WH + 1))
            x2_sig = []
            for fc in range(FC):
                par = fc % 2
                half = fc // 4
                iz, pz, wlz = uz[half]
                ih, ph, wlh = uh[half]
                bk, bke = banks.acq()
                bt, bte = banks.acq()
                for kc in range(FC):
                    incs = [(T, 1)] if kc == 7 else []
                    B.em("tensor", lambda t, b=bk, k=kc, p=pz, tt=fc % 4: t.matmul(
                        bankap(b), wslice(p, k, tt),
                        xnb[:, k * 512:(k + 1) * 512], start=(k == 0), stop=(k == 7)),
                        waits=(bke if kc == 0 else []) + ([wlz] if kc == 0 else []) +
                              [xnb_sig[kc]],
                        incs=incs)
                if fc % 4 == 3:
                    ring.freed(iz, full_order)
                for kc in range(FC):
                    incs = [(T, 1)] if kc == 7 else []
                    B.em("tensor", lambda t, b=bt, k=kc, p=ph, tt=fc % 4: t.matmul(
                        bankap(b), wslice(p, k, tt),
                        xnb[:, k * 512:(k + 1) * 512], start=(k == 0), stop=(k == 7)),
                        waits=(bte if kc == 0 else []) + ([wlh] if kc == 0 else []),
                        incs=incs)
                if fc % 4 == 3:
                    ring.freed(ih, full_order)
                kp = T.n
                pw = [(T, kp), (V, atrk["scan_v"][par]), (G, atrk["b_g"][par])]
                B.em("scalar", lambda s, b=bk, f=fc, p=par: s.activation(
                    at(p, 0), bankap(b), AF.Sigmoid, bias=nbz[:, f:f + 1], scale=-1.0),
                    waits=pw, incs=[(SS, 1)])
                a_ss = SS.n
                B.em("scalar", lambda s, b=bt, f=fc, p=par: s.activation(
                    at(p, 2), bankap(b), AF.Sigmoid, bias=bh[:, f:f + 1], scale=1.0))
                B.em("scalar", lambda s, b=bt, f=fc, p=par: s.activation(
                    at(p, 3), bankap(b), AF.Relu, bias=bh[:, f:f + 1], scale=1.0),
                    incs=[(SS, 1)])
                r_ss = SS.n
                banks.release(bk, (SS, a_ss))
                banks.release(bt, (SS, r_ss))
                B.em("vector", lambda v, p=par: v.tensor_scalar(
                    at(p, 1), at(p, 0), -1.0, 1.0, ALU.mult, ALU.add),
                    waits=[(SS, a_ss), (G, atrk["b_g"][par])])
                B.em("vector", lambda v, p=par: v.scalar_tensor_tensor(
                    at(p, 2), at(p, 2), 0.5, at(p, 3), ALU.min, ALU.add),
                    waits=[(SS, r_ss)], incs=[(V, 1)])
                gt_v = V.n
                B.em("gpsimd", lambda g, p=par: g.tensor_mul(
                    at(p, 3), at(p, 1), at(p, 2)),
                    waits=[(V, gt_v)], incs=[(G, 1)])
                atrk["b_g"][par] = G.n
                b_g = G.n
                B.em("vector", lambda v: v.drain())
                for s_ in range(SEQ):
                    B.em("vector", lambda v, f=fc, ss=s_, p=par: v.tensor_tensor_scan(
                        xf[:, f * 512 + ss * 128: f * 512 + (ss + 1) * 128],
                        at(p, 0)[:, ss * 128:(ss + 1) * 128],
                        at(p, 3)[:, ss * 128:(ss + 1) * 128],
                        carry[:, f * 4 + ss: f * 4 + ss + 1],
                        ALU.mult, ALU.add),
                        waits=([(G, b_g), (SS, atrk["carry_ss"][fc])]
                               if s_ == 0 else []),
                        incs=([(V, 1)] if s_ == SEQ - 1 else []))
                atrk["scan_v"][par] = V.n
                sc_v = V.n
                B.em("scalar", lambda s, f=fc: s.activation(
                    carry[:, f * 4: f * 4 + 4],
                    xf.ap()[:, f * 512:(f + 1) * 512].rearrange(
                        "p (s t) -> p s t", s=4)[:, :, 127],
                    AF.Identity, bias=0.0, scale=1.0),
                    waits=[(V, sc_v)], incs=[(SS, 1)])
                atrk["carry_ss"][fc] = SS.n
                B.em("gpsimd", lambda g, f=fc, xs=xsrc, xd=xdst: g.tensor_add(
                    xd[:, f * 512:(f + 1) * 512], xs[:, f * 512:(f + 1) * 512],
                    xf[:, f * 512:(f + 1) * 512]),
                    waits=[(V, sc_v)], incs=[(G, 1)])
                x2_sig.append((G, G.n))
            return x2_sig

        state = {"in_tg": 0}

        def emit_in_dmas(ci):
            t0 = ci * TLOC
            for pp_ in range(2):
                B.em("gpsimd", lambda g, tt=t0, pp=pp_: g.dma_start(
                    out=stage.ap()[:, pp * 2048:(pp + 1) * 2048].rearrange(
                        "p (s d) -> p s d", s=2),
                    in_=x_in[2 * pp:2 * pp + 2, tt:tt + TLOC, :].rearrange(
                        "s t d -> t s d")),
                    waits=[(T, state["in_tg"])], incs=[(XL, 16)])
            return XL.n

        def emit_in_passes(xl_v, x3_guard):
            x0_sig = [None] * FC
            for dc in range(FC):
                vn = None
                for s_ in range(SEQ):
                    bi, be = banks.acq()
                    B.em("tensor", lambda t, b=bi, ss=s_, d=dc: t.transpose(
                        bankap(b, (slice(0, 128), slice(0, 128))),
                        stage[:, ss * 1024 + d * 128: ss * 1024 + (d + 1) * 128],
                        idt[:, :]),
                        waits=be + [(XL, xl_v)],
                        incs=[(T, 1)])
                    guard = [x3_guard[dc]] if x3_guard is not None else []
                    if s_ % 2 == 0:
                        B.em("vector", lambda v, b=bi, ss=s_, d=dc: v.tensor_copy(
                            xA[:, d * 512 + ss * 128: d * 512 + (ss + 1) * 128],
                            bankap(b, (slice(0, 128), slice(0, 128)))),
                            waits=[cap(T)] + guard, incs=[(V, 1)])
                        vn = (V, V.n)
                        banks.release(bi, cap(V))
                    else:
                        B.em("scalar", lambda s, b=bi, ss=s_, d=dc: s.activation(
                            xA[:, d * 512 + ss * 128: d * 512 + (ss + 1) * 128],
                            bankap(b, (slice(0, 128), slice(0, 128))),
                            AF.Identity, bias=0.0, scale=1.0),
                            waits=[cap(T)] + guard, incs=[(SS, 1)])
                        banks.release(bi, cap(SS))
                x0_sig[dc] = [vn, (SS, SS.n)]
            state["in_tg"] = T.n
            return x0_sig

        def emit_out_group(ci_prev, tt, ph, x3s):
            for dc in range(FC):
                bi, be = banks.acq()
                B.em("tensor", lambda t, b=bi, ss=ph, d=dc: t.transpose(
                    bankap(b, (slice(0, 128), slice(0, 128))),
                    xB[:, d * 512 + ss * 128: d * 512 + (ss + 1) * 128],
                    idt[:, :]),
                    waits=be + [x3s[dc]],
                    incs=[(T, 1)])
                if dc % 2 == 0:
                    B.em("vector", lambda v, b=bi, d=dc: v.tensor_copy(
                        stage[:, 4096 + d * 128: 4096 + (d + 1) * 128],
                        bankap(b, (slice(0, 128), slice(0, 128)))),
                        waits=[cap(T), (OG, 64 * ci_prev + 16 * ph)],
                        incs=[(V, 1)])
                    banks.release(bi, cap(V))
                else:
                    B.em("scalar", lambda s, b=bi, d=dc: s.activation(
                        stage[:, 4096 + d * 128: 4096 + (d + 1) * 128],
                        bankap(b, (slice(0, 128), slice(0, 128))),
                        AF.Identity, bias=0.0, scale=1.0),
                        waits=[cap(T), (OG, 64 * ci_prev + 16 * ph)],
                        incs=[(SS, 1)])
                    banks.release(bi, cap(SS))
            B.em("gpsimd", lambda g, tt=tt, pp=ph: g.dma_start(
                out=y_out[pp:pp + 1, tt:tt + TLOC, :].rearrange(
                    "s t d -> t s d"),
                in_=stage.ap()[:, 4096:5120].rearrange("p (s d) -> p s d", s=1)),
                waits=[(V, V.n), (SS, SS.n)], incs=[(OG, 16)])

        pending_in = None
        x3_prev = None
        prev_out = None
        for ci in range(n_chunks):
            t0 = ci * TLOC
            if pending_in is None:
                B.em("vector", lambda v: v.drain())
                xl_v = emit_in_dmas(ci)
                x0_sig = emit_in_passes(xl_v, None)
            else:
                x0_sig = emit_in_passes(pending_in, x3_prev)
            if prev_out is not None:
                p_ci, p_t0, p_x3 = prev_out
                fillers = [lambda ph=ph, a=p_ci, b=p_t0, c=p_x3:
                           emit_out_group(a, b, ph, c)
                           for ph in range(4)]
            else:
                fillers = None
            xf_sig, xnb_sig = layernorm(xA, x0_sig, LN_EPS)
            xf_sig, xnb_sig = ffn_iter(0, xf_sig, xnb_sig, fillers=fillers)
            xf_sig, xnb_sig = ffn_iter(1, xf_sig, xnb_sig)
            B.em("vector", lambda v: v.drain())
            x1_sig = []
            for fc in range(FC):
                B.em("vector", lambda v, f=fc: v.scalar_tensor_tensor(
                    xB[:, f * 512:(f + 1) * 512], xf[:, f * 512:(f + 1) * 512],
                    0.5, xA[:, f * 512:(f + 1) * 512], ALU.mult, ALU.add),
                    waits=[xf_sig[fc], cap(SS)], incs=[(V, 1)])
                x1_sig.append((V, V.n))
            xf_sig, xnb_sig = layernorm(xB, x1_sig, LN_EPS)
            x2_sig = attn(xB, xA, xnb_sig)
            xf_sig, xnb_sig = layernorm(xA, x2_sig, LN_EPS)
            xf_sig, xnb_sig = ffn_iter(0, xf_sig, xnb_sig)
            xf_sig, xnb_sig = ffn_iter(1, xf_sig, xnb_sig)
            B.em("vector", lambda v: v.drain())
            x3_sig = []
            for fc in range(FC):
                B.em("vector", lambda v, f=fc: v.scalar_tensor_tensor(
                    xB[:, f * 512:(f + 1) * 512], xf[:, f * 512:(f + 1) * 512],
                    0.5, xA[:, f * 512:(f + 1) * 512], ALU.mult, ALU.add),
                    waits=[xf_sig[fc], cap(SS)], incs=[(V, 1)])
                x3_sig.append((V, V.n))
            if ci + 1 < n_chunks:
                pending_in = emit_in_dmas(ci + 1)
                x3_prev = x3_sig
            prev_out = (ci, t0, x3_sig)

        p_ci, p_t0, p_x3 = prev_out
        for ph in range(4):
            emit_out_group(p_ci, p_t0, ph, p_x3)

        B.em("gpsimd", lambda g: g.nop(), waits=[(OG, 64 * n_chunks)])

        assert ring.use_i == len(full_order), (ring.use_i, len(full_order))

        with nc.Block() as block:
            B.replay(block)
    return nc


def kernel(**inputs):
    nc = build_nc(NCHUNK_FULL)
    in_maps = [host_inputs(inputs, c) for c in range(8)]
    res = run_bass_kernel_spmd(nc, in_maps, core_ids=list(range(8)))
    out = np.concatenate([r["y_out"] for r in res.results], axis=0)
    return out.astype(np.float32)


# version 19
# speedup vs baseline: 1.1266x; 1.1266x over previous
import contextlib
import numpy as np

import concourse.bass as bass
import concourse.mybir as mybir
from concourse.bass_utils import run_bass_kernel_spmd

F32 = mybir.dt.float32
BF16 = mybir.dt.bfloat16
AF = mybir.ActivationFunctionType
ALU = mybir.AluOpType
AX = mybir.AxisListType.X

D = 1024
E = 6
FC = 8
NT = 512
SEQ = 4
TLOC = 128
NCHUNK_FULL = 16
RING = 6
LN_EPS = 1e-5
RMS_EPS = 1e-6

SLOT_SUP = 36
SLOT_SG = 40
SLOT_SD = 42
SLOT_WZ = 44
SLOT_WH = 46
NSLOT = 48


def _halves(W):
    W = np.asarray(W, dtype=np.float32)
    K, M = W.shape
    assert K == 1024 and M % 512 == 0
    out = []
    for c in range(0, M, 512):
        H = W[:, c:c + 512].reshape(8, 128, 512).transpose(1, 0, 2).reshape(128, 4096)
        out.append(np.ascontiguousarray(H))
    return out


def _pcol(v):
    return np.ascontiguousarray(np.asarray(v, np.float32).reshape(8, 128).T)


def host_inputs(ins, core_id):
    x = np.ascontiguousarray(ins["L0_attn_h_prev_layer"][4 * core_id:4 * core_id + 4])

    halves = []
    for e in range(E):
        halves += _halves(ins["L0_ffn_exp_Wup"][e])
        halves += _halves(ins["L0_ffn_exp_Wg"][e])
        halves += _halves(ins["L0_ffn_exp_Wdown"][e])
    halves += _halves(ins["L0_ffn_share_Wup"])
    halves += _halves(ins["L0_ffn_share_Wg"])
    halves += _halves(ins["L0_ffn_share_Wdown"])
    halves += _halves(np.asarray(ins["L0_attn_W_z"], np.float32).T)
    halves += _halves(np.asarray(ins["L0_attn_W_h"], np.float32).T)
    wblob = np.stack(halves)

    bup = np.asarray(ins["L0_ffn_exp_bup"], np.float32).reshape(E, 8, 128)
    bup = np.ascontiguousarray(bup.transpose(2, 0, 1).reshape(128, E * 8))
    bg = np.asarray(ins["L0_ffn_exp_bg"], np.float32).reshape(E, 8, 128)
    bg = np.ascontiguousarray(bg.transpose(2, 0, 1).reshape(128, E * 8))
    nw = np.asarray(ins["L0_ffn_norm_w"], np.float32).reshape(2, 8, 128)
    nw = np.ascontiguousarray(nw.transpose(2, 0, 1).reshape(128, 16))
    rw = np.asarray(ins["L0_ffn_router_W"], np.float32).reshape(2, 8, 128, E)
    rw = np.ascontiguousarray(rw.transpose(2, 0, 1, 3).reshape(128, 2 * 8 * E))

    return {
        "x_in": x,
        "wblob": wblob,
        "t_bz": _pcol(ins["L0_attn_b_z"]),
        "t_bh": _pcol(ins["L0_attn_b_h"]),
        "t_bup": bup,
        "t_bg": bg,
        "t_bsup": np.ascontiguousarray(
            np.asarray(ins["L0_ffn_share_bup"], np.float32).reshape(16, 128).T),
        "t_bsg": _pcol(ins["L0_ffn_share_bg"]),
        "t_bsd": _pcol(ins["L0_ffn_share_bdown"]),
        "t_nw": nw,
        "t_rw": rw,
        "t_rb": np.ascontiguousarray(np.asarray(ins["L0_ffn_router_b"], np.float32).T),
        "t_gb": np.ascontiguousarray(np.asarray(ins["L0_ffn_gate_bias"], np.float32).T),
        "t_bdn": np.ascontiguousarray(np.asarray(ins["L0_ffn_exp_bdown"], np.float32)),
        "t_id": np.eye(128, dtype=np.float32),
        "t_oh": np.ascontiguousarray(
            np.repeat(np.eye(6, dtype=np.float32), 128, axis=1)),
    }


class Sem:
    def __init__(self, h):
        self.h = h
        self.n = 0


class Builder:
    ENGS = ("sync", "tensor", "vector", "scalar", "gpsimd")

    def __init__(self):
        self.ops = {e: [] for e in self.ENGS}
        self.last_wait = {e: {} for e in self.ENGS}

    def em(self, eng, fn, waits=(), incs=()):
        lw = self.last_wait[eng]
        ws = []
        for s, v in waits:
            if v <= 0:
                continue
            if lw.get(id(s), -1) >= v:
                continue
            lw[id(s)] = v
            ws.append((s, v))
        merged = {}
        for s, a in incs:
            merged[id(s)] = (s, merged.get(id(s), (s, 0))[1] + a)
        incs = list(merged.values())
        assert len(incs) <= 1, f"multiple sems inc'd by one op: {incs}"
        for s, a in incs:
            s.n += a
        self.ops[eng].append((tuple(ws), fn, tuple(incs)))

    def replay(self, block):
        for eng in self.ENGS:
            dec = getattr(block, eng)
            ops = self.ops[eng]

            def make(ops):
                def _body(engobj):
                    for ws, fn, incs in ops:
                        for s, v in ws:
                            engobj.wait_ge(s.h, v)
                        ins = fn(engobj)
                        for s, a in incs:
                            ins = ins.then_inc(s.h, a)
                return _body
            dec(make(ops))


def cap(sem):
    return (sem, sem.n)


class Banks:

    def __init__(self, tiles):
        self.t = tiles
        self.state = [[] for _ in tiles]
        self.rr = 0

    def acq(self):
        n = len(self.t)
        for k in range(n):
            i = (self.rr + k) % n
            if self.state[i] != "inuse":
                edges = list(self.state[i])
                self.state[i] = "inuse"
                self.rr = i + 1
                return i, edges
        raise RuntimeError("all PSUM banks in use")

    def release(self, i, edge):
        if self.state[i] == "inuse":
            self.state[i] = [edge]
        else:
            self.state[i].append(edge)


def build_nc(n_chunks=NCHUNK_FULL):
    nc = bass.Bass("TRN2", target_bir_lowering=False, debug=False)

    x_in = nc.dram_tensor("x_in", [SEQ, 2048, D], F32, kind="ExternalInput").ap()
    wblob = nc.dram_tensor("wblob", [NSLOT, 128, 4096], F32, kind="ExternalInput").ap()
    names_small = ["t_bz", "t_bh", "t_bup", "t_bg", "t_bsup", "t_bsg", "t_bsd",
                   "t_nw", "t_rw", "t_rb", "t_gb", "t_bdn", "t_id", "t_oh"]
    shapes_small = {"t_bz": [128, 8], "t_bh": [128, 8], "t_bup": [128, 48],
                    "t_bg": [128, 48], "t_bsup": [128, 16], "t_bsg": [128, 8],
                    "t_bsd": [128, 8], "t_nw": [128, 16], "t_rw": [128, 96],
                    "t_rb": [6, 2], "t_gb": [6, 2], "t_bdn": [6, 1024],
                    "t_id": [128, 128], "t_oh": [6, 768]}
    dsmall = {n: nc.dram_tensor(n, shapes_small[n], F32, kind="ExternalInput").ap()
              for n in names_small}
    y_out = nc.dram_tensor("y_out", [SEQ, 2048, D], F32, kind="ExternalOutput").ap()
    wscr = nc.dram_tensor("wscr", [NSLOT, 128, 4096], BF16, kind="Internal").ap()

    ctx = contextlib.ExitStack()
    with ctx:
        def sb(name, shape, dt):
            return ctx.enter_context(nc.sbuf_tensor(name, shape, dt))

        wring = sb("wring", [128, RING * 4096], BF16)
        xA = sb("xA", [128, 4096], F32)
        xB = sb("xB", [128, 4096], F32)
        xf = sb("xf", [128, 4096], F32)
        xnb = sb("xnb", [128, 4096], BF16)
        hb2 = sb("hb2", [128, 8192], BF16)
        yac = sb("yac", [128, 4096], F32)
        prod = sb("prod", [128, 4096], BF16)
        sgt = sb("sgt", [128, 1024], F32)
        p1t = sb("p1t", [128, 1024], F32)
        tnt = sb("tnt", [128, 1024], F32)
        stage = sb("stage", [128, 6144], F32)
        rowA = sb("rowA", [1, 512], F32)
        rowB = sb("rowB", [1, 512], F32)
        Rs = sb("Rs", [128, 512], F32)
        MRs = sb("MRs", [128, 512], F32)
        rt_b6 = sb("rt_b6", [6, 512], F32)
        rt_g6 = sb("rt_g6", [6, 512], F32)
        w6b = sb("w6b", [6, 512], BF16)
        rt_bt = sb("rt_bt", [128, 24], F32)
        rt_gt = sb("rt_gt", [128, 24], F32)
        r_eq = sb("r_eq", [128, 24], F32)
        r_b2 = sb("r_b2", [128, 24], F32)
        r_sel = sb("r_sel", [128, 24], F32)
        r_gs = sb("r_gs", [128, 24], F32)
        r_d = sb("r_d", [128, 24], F32)
        r_wp = sb("r_wp", [128, 24], F32)
        r_w = sb("r_w", [128, 24], F32)
        r_m1 = sb("r_m1", [128, 4], F32)
        r_m2 = sb("r_m2", [128, 4], F32)
        r_S = sb("r_S", [128, 4], F32)
        carry = sb("carry", [128, 32], F32)
        bz = sb("bz", [128, 8], F32)
        nbz = sb("nbz", [128, 8], F32)
        bh = sb("bh", [128, 8], F32)
        bup = sb("bup", [128, 48], F32)
        bgt_ = sb("bgt", [128, 48], F32)
        bsup = sb("bsup", [128, 16], F32)
        bsg = sb("bsg", [128, 8], F32)
        bsd = sb("bsd", [128, 8], F32)
        csd = sb("csd", [128, 8], F32)
        nw = sb("nw", [128, 16], F32)
        rw = sb("rw", [128, 96], F32)
        rb = sb("rb", [6, 2], F32)
        gb = sb("gb", [6, 2], F32)
        bdn = sb("bdn", [6, 1024], BF16)
        oh6 = sb("oh6", [6, 768], BF16)
        idt = sb("idt", [128, 128], F32)
        c_eps_ln = sb("c_eps_ln", [1, 1], F32)
        c_eps_rms = sb("c_eps_rms", [1, 1], F32)
        nc.const_aps.register(F32, LN_EPS, c_eps_ln.ap())
        nc.const_aps.register(F32, RMS_EPS, c_eps_rms.ap())
        nwrow8 = sb("nwrow8", [1, 1024], F32)
        ones_bc = sb("ones_bc", [128, 1], BF16)
        ones_fr = sb("ones_fr", [1, 128], F32)

        bank_t = [ctx.enter_context(nc.psum_tensor(f"pb{i}", [128, 512], F32))
                  for i in range(8)]

        S = {n: Sem(ctx.enter_context(nc.semaphore(n))) for n in
             ("BL", "XL", "OG", "OH", "T", "V", "SS", "G")}
        BL, XL, OG = S["BL"], S["XL"], S["OG"]
        OGs = [S["OG"], S["OH"]]
        T, V, SS, G = S["T"], S["V"], S["SS"], S["G"]
        WLp = [Sem(ctx.enter_context(nc.semaphore(f"WL{i}"))) for i in range(RING)]
        CVg = [Sem(ctx.enter_context(nc.semaphore(f"CV{i}"))) for i in range(6)]
        CV_GROUPS = [2, 4, 6, 12, 12, 12]
        CV_OF = []
        for gi, n_ in enumerate(CV_GROUPS):
            CV_OF += [gi] * n_

        B = Builder()
        banks = Banks(bank_t)

        def bankap(i, sl=None):
            return bank_t[i][:, :] if sl is None else bank_t[i][sl[0], sl[1]]

        sb_map = {"t_bz": bz, "t_bh": bh, "t_bup": bup, "t_bg": bgt_,
                  "t_bsup": bsup, "t_bsg": bsg, "t_bsd": bsd, "t_nw": nw,
                  "t_rw": rw, "t_rb": rb, "t_gb": gb, "t_id": idt}
        for n, tile in sb_map.items():
            B.em("gpsimd", lambda g, t=tile, a=dsmall[n]: g.dma_start(out=t[:, :], in_=a),
                 incs=[(BL, 16)])
        B.em("gpsimd", lambda g: g.dma_start(out=bdn[:, :], in_=dsmall["t_bdn"]),
             incs=[(BL, 16)])
        B.em("gpsimd", lambda g: g.dma_start(out=oh6[:, :], in_=dsmall["t_oh"]),
             incs=[(BL, 16)])
        bl_total = BL.n

        for s_ in range(NSLOT):
            B.em("gpsimd", lambda g, s=s_: g.dma_start(out=wscr[s], in_=wblob[s]),
                 incs=[(CVg[CV_OF[s_]], 16)])
        conv_done = [(CVg[CV_OF[s]], 16 * CV_GROUPS[CV_OF[s]]) for s in range(NSLOT)]

        B.em("vector", lambda v: v.memset(ones_bc[:, :], 1.0), waits=[cap(BL)])
        B.em("vector", lambda v: v.memset(c_eps_ln[:, :], LN_EPS))
        B.em("vector", lambda v: v.memset(c_eps_rms[:, :], RMS_EPS))
        B.em("vector", lambda v: v.memset(ones_fr[:, :], 1.0))
        B.em("vector", lambda v: v.memset(carry[:, :], 0.0))
        B.em("vector", lambda v: v.tensor_scalar_mul(nbz[:, :], bz[:, :], -1.0))
        B.em("vector", lambda v: v.tensor_scalar_mul(csd[:, :], bsd[:, :], 2.0),
             incs=[(V, 1)])
        pre_v = V.n

        B.em("tensor", lambda t: t.nop(), waits=[(BL, bl_total), (V, pre_v)])
        B.em("scalar", lambda s: s.nop(), waits=[(BL, bl_total), (V, pre_v)])
        B.em("gpsimd", lambda g: g.nop(), waits=[(BL, bl_total)])

        class Ring:
            def __init__(self):
                self.load_i = 0
                self.use_i = 0
                self.load_wl = []
                self.free_wf = []

            def pump(self, order):
                while (self.load_i < min(len(order), self.use_i + RING)
                       and (self.load_i < RING
                            or self.free_wf[self.load_i - RING] is not None)):
                    li = self.load_i
                    slot = order[li]
                    pos = li % RING
                    waits = []
                    if li >= RING:
                        waits.append((T, self.free_wf[li - RING]))
                    waits.append(conv_done[slot])
                    B.em("sync",
                         lambda s, p=pos, sl=slot: s.dma_start(
                             out=wring[:, p * 4096:(p + 1) * 4096], in_=wscr[sl]),
                         waits=waits, incs=[(WLp[pos], 16)])
                    self.load_wl.append((WLp[pos], WLp[pos].n))
                    self.free_wf.append(None)
                    self.load_i += 1

            def use(self, order, expect_slot):
                i = self.use_i
                assert order[i] == expect_slot, (i, order[i], expect_slot)
                self.pump(order)
                self.use_i += 1
                pos = i % RING
                return i, pos, self.load_wl[i]

            def freed(self, i, order):
                self.free_wf[i] = T.n
                self.pump(order)

        ring = Ring()

        def blk_up(k):
            if k < E:
                return [6 * k, 6 * k + 1]
            return ([SLOT_SUP, SLOT_SUP + 1] if k == E
                    else [SLOT_SUP + 2, SLOT_SUP + 3])

        def blk_gate(k):
            return [6 * k + 2, 6 * k + 3] if k < E else [SLOT_SG, SLOT_SG + 1]

        def blk_down(k):
            return [6 * k + 4, 6 * k + 5] if k < E else [SLOT_SD, SLOT_SD + 1]

        it_order = blk_up(0) + blk_up(1)
        for k in range(8):
            if k != 7:
                it_order += blk_gate(k)
            if k + 2 < 8:
                it_order += blk_up(k + 2)
            if k != 7:
                it_order += blk_down(k)
        attn_order = [SLOT_WZ, SLOT_WH, SLOT_WZ + 1, SLOT_WH + 1]
        chunk_order = it_order * 2 + attn_order + it_order * 2
        full_order = chunk_order * n_chunks

        def wslice(pos, kc, t):
            return wring[:, pos * 4096 + kc * 512 + t * 128:
                         pos * 4096 + kc * 512 + t * 128 + 128]

        def v3(ap, g):
            return ap.rearrange("p (g e) -> p g e", g=g)

        def vo1(ap):
            return ap.rearrange("p (g o) -> p g o", o=1)

        qg = {"V0": 0, "V1": 0}

        def stats_and_rows(x_sig, is_ln, xbuf, eps):
            qview = hb2[:, 0:4096]
            sq_sig = []
            cast_sig = []
            for fc in range(FC):
                xs = x_sig[fc] if isinstance(x_sig[fc], list) else [x_sig[fc]]
                B.em("scalar", lambda s, f=fc, xb=xbuf: s.activation(
                    qview[:, f * 512:(f + 1) * 512], xb[:, f * 512:(f + 1) * 512],
                    AF.Square, bias=0.0, scale=1.0),
                    waits=xs + [cap(T), cap(G), (V, qg["V0"])], incs=[(SS, 1)])
                sq_sig.append((SS, SS.n))
                if is_ln:
                    B.em("scalar", lambda s, f=fc, xb=xbuf: s.activation(
                        xnb[:, f * 512:(f + 1) * 512], xb[:, f * 512:(f + 1) * 512],
                        AF.Identity, bias=0.0, scale=1.0),
                        waits=xs + [cap(T)], incs=[(SS, 1)])
                    cast_sig.append((SS, SS.n))
            bsq, bsq_e = banks.acq()
            for kc in range(FC):
                B.em("tensor", lambda t, b=bsq, k=kc: t.matmul(
                    bankap(b, (slice(0, 1), slice(0, 512))), ones_bc[:, :],
                    qview[:, k * 512:(k + 1) * 512], start=(k == 0), stop=(k == 7)),
                    waits=(bsq_e if kc == 0 else []) + [sq_sig[kc]],
                    incs=([(T, 1)] if kc == 7 else []))
            if is_ln:
                bsum, bsum_e = banks.acq()
                for kc in range(FC):
                    B.em("tensor", lambda t, b=bsum, k=kc: t.matmul(
                        bankap(b, (slice(0, 1), slice(0, 512))), ones_bc[:, :],
                        xnb[:, k * 512:(k + 1) * 512], start=(k == 0), stop=(k == 7)),
                        waits=(bsum_e if kc == 0 else []) + [cast_sig[kc]],
                        incs=([(T, 1)] if kc == 7 else []))
                B.em("vector", lambda v, b=bsum: v.tensor_scalar_mul(
                    rowA[:, :], bankap(b, (slice(0, 1), slice(0, 512))), 1.0 / D),
                    waits=[cap(T)], incs=[(V, 1)])
                B.em("vector", lambda v: v.tensor_mul(
                    rowB[:, :], rowA[:, :], rowA[:, :]),
                    waits=[(V, V.n)], incs=[(V, 1)])
                B.em("vector", lambda v, b=bsq: v.scalar_tensor_tensor(
                    rowB[:, :], bankap(b, (slice(0, 1), slice(0, 512))), 1.0 / D,
                    rowB[:, :], ALU.mult, ALU.subtract),
                    waits=[(V, V.n)], incs=[(V, 1)])
                banks.release(bsum, cap(V))
                banks.release(bsq, cap(V))
                B.em("scalar", lambda s, e=eps: s.activation(
                    rowB[:, :], rowB[:, :], AF.Sqrt, bias=e, scale=1.0),
                    waits=[cap(V)], incs=[(SS, 1)])
                B.em("vector", lambda v: v.reciprocal(rowB[:, :], rowB[:, :]),
                     waits=[cap(SS)], incs=[(V, 1)])
                rstd_sig = (V, V.n)
                B.em("vector", lambda v: v.tensor_mul(
                    rowA[:, :], rowA[:, :], rowB[:, :]),
                    waits=[(V, V.n)], incs=[(V, 1)])
                rowA_sig = (V, V.n)
            else:
                B.em("scalar", lambda s, b=bsq, e=eps: s.activation(
                    rowB[:, :], bankap(b, (slice(0, 1), slice(0, 512))), AF.Sqrt,
                    bias=e, scale=1.0 / D),
                    waits=[cap(T)], incs=[(SS, 1)])
                banks.release(bsq, cap(SS))
                B.em("vector", lambda v: v.reciprocal(rowB[:, :], rowB[:, :]),
                     waits=[cap(SS)], incs=[(V, 1)])
                rstd_sig = (V, V.n)
                rowA_sig = None
            return rstd_sig, rowA_sig

        def bcast_to_sbuf(row_tile, dst, src_sig):
            bi, be = banks.acq()
            B.em("tensor", lambda t, b=bi, r=row_tile: t.matmul(
                bankap(b), ones_fr[:, :], r[:, :], start=True, stop=True),
                waits=be + [src_sig], incs=[(T, 1)])
            B.em("vector", lambda v, b=bi, d=dst: v.tensor_copy(d[:, :], bankap(b)),
                 waits=[cap(T), cap(G)], incs=[(V, 1)])
            banks.release(bi, cap(V))

        def layernorm(xbuf, x_sig, eps):
            rstd_sig, rowA_sig = stats_and_rows(x_sig, True, xbuf, eps)
            bcast_to_sbuf(rowB, Rs, rstd_sig)
            bcast_to_sbuf(rowA, MRs, rowA_sig)
            r_v = V.n
            xf_sig, xnb_sig = [], []
            for fc in range(FC):
                B.em("vector", lambda v, f=fc, xb=xbuf: v.tensor_tensor(
                    tnt[:, (f % 2) * 512:(f % 2 + 1) * 512],
                    xb[:, f * 512:(f + 1) * 512],
                    Rs[:, :], ALU.mult),
                    waits=[(V, r_v if fc < 2 else xf_sig[fc - 2][1]), cap(G)],
                    incs=[(V, 1)])
                tv = V.n
                B.em("vector", lambda v, f=fc: v.tensor_sub(
                    xf[:, f * 512:(f + 1) * 512],
                    tnt[:, (f % 2) * 512:(f % 2 + 1) * 512],
                    MRs[:, :]),
                    waits=[(V, tv)], incs=[(V, 1)])
                xf_sig.append((V, V.n))
                B.em("scalar", lambda s, f=fc: s.activation(
                    xnb[:, f * 512:(f + 1) * 512], xf[:, f * 512:(f + 1) * 512],
                    AF.Identity, bias=0.0, scale=1.0),
                    waits=[(V, V.n)], incs=[(SS, 1)])
                xnb_sig.append((SS, SS.n))
            return xf_sig, xnb_sig

        def rms(j, yac_sig):
            rstd_sig, _ = stats_and_rows(yac_sig, False, yac, RMS_EPS)
            bcast_to_sbuf(rowB, Rs, rstd_sig)
            r_v = V.n
            xf_sig, xnb_sig = [], []
            for fc in range(FC):
                B.em("vector", lambda v, f=fc, jj=j: v.tensor_single_scalar(
                    tnt[:, (f % 2) * 512:(f % 2 + 1) * 512],
                    yac[:, f * 512:(f + 1) * 512],
                    nw[:, jj * 8 + f:jj * 8 + f + 1], ALU.mult),
                    waits=[(V, r_v), cap(G)], incs=[(V, 1)])
                tv = V.n
                B.em("gpsimd", lambda g, f=fc: g.tensor_mul(
                    xf[:, f * 512:(f + 1) * 512],
                    tnt[:, (f % 2) * 512:(f % 2 + 1) * 512],
                    Rs[:, :]),
                    waits=[(V, tv)], incs=[(G, 1)])
                xf_sig.append((G, G.n))
                B.em("scalar", lambda s, f=fc: s.activation(
                    xnb[:, f * 512:(f + 1) * 512], xf[:, f * 512:(f + 1) * 512],
                    AF.Identity, bias=0.0, scale=1.0),
                    waits=[(G, G.n)], incs=[(SS, 1)])
                xnb_sig.append((SS, SS.n))
            return xf_sig, xnb_sig

        def router_mm(j, xf_sig):
            bi, be = banks.acq()
            for kc in range(FC):
                B.em("tensor", lambda t, b=bi, k=kc, jj=j: t.matmul(
                    bankap(b, (slice(0, 6), slice(0, 512))),
                    rw[:, jj * 48 + k * 6: jj * 48 + (k + 1) * 6],
                    xf[:, k * 512:(k + 1) * 512], start=(k == 0), stop=(k == 7)),
                    waits=(be if kc == 0 else []) + [xf_sig[kc]],
                    incs=([(T, 1)] if kc == 7 else []))
            B.em("vector", lambda v, b=bi, jj=j: v.tensor_scalar(
                rt_b6[:, :], bankap(b, (slice(0, 6), slice(0, 512))),
                rb[:, jj:jj + 1], gb[:, jj:jj + 1], ALU.add, ALU.add),
                waits=[cap(T)])
            B.em("vector", lambda v, b=bi, jj=j: v.tensor_single_scalar(
                rt_g6[:, :], bankap(b, (slice(0, 6), slice(0, 512))),
                rb[:, jj:jj + 1], ALU.add), incs=[(V, 1)])
            banks.release(bi, cap(V))
            return V.n

        def router_transposes(rt_v):
            bt1, bt1e = banks.acq()
            bt2, bt2e = banks.acq()
            for grp in range(4):
                B.em("tensor", lambda t, b=bt1, g_=grp: t.transpose(
                    bankap(b, (slice(0, 128), slice(g_ * 6, (g_ + 1) * 6))),
                    rt_b6[:, g_ * 128:(g_ + 1) * 128], idt[0:6, 0:6]),
                    waits=(bt1e if grp == 0 else []) + [(V, rt_v)])
            for grp in range(4):
                B.em("tensor", lambda t, b=bt2, g_=grp: t.transpose(
                    bankap(b, (slice(0, 128), slice(g_ * 6, (g_ + 1) * 6))),
                    rt_g6[:, g_ * 128:(g_ + 1) * 128], idt[0:6, 0:6]),
                    waits=(bt2e if grp == 0 else []),
                    incs=([(T, 1)] if grp == 3 else []))
            B.em("vector", lambda v, b=bt1: v.tensor_copy(
                rt_bt[:, :], bankap(b, (slice(0, 128), slice(0, 24)))),
                waits=[cap(T)])
            B.em("vector", lambda v, b=bt2: v.tensor_copy(
                rt_gt[:, :], bankap(b, (slice(0, 128), slice(0, 24)))),
                incs=[(V, 1)])
            banks.release(bt1, cap(V))
            banks.release(bt2, cap(V))
            B.em("vector", lambda v: v.drain())
            B.em("vector", lambda v: v.reduce_max(vo1(r_m1.ap()), v3(rt_bt.ap(), 4), AX))
            B.em("vector", lambda v: v.drain())
            for g_ in range(4):
                B.em("vector", lambda v, gg=g_: v.tensor_single_scalar(
                    r_eq[:, gg * 6:(gg + 1) * 6], rt_bt[:, gg * 6:(gg + 1) * 6],
                    r_m1[:, gg:gg + 1], ALU.is_ge))
            B.em("vector", lambda v: v.drain())
            for g_ in range(4):
                B.em("vector", lambda v, gg=g_: v.scalar_tensor_tensor(
                    r_b2[:, gg * 6:(gg + 1) * 6], r_eq[:, gg * 6:(gg + 1) * 6],
                    -30000.0, rt_bt[:, gg * 6:(gg + 1) * 6], ALU.mult, ALU.add))
            B.em("vector", lambda v: v.drain())
            B.em("vector", lambda v: v.reduce_max(vo1(r_m2.ap()), v3(r_b2.ap(), 4), AX))
            B.em("vector", lambda v: v.drain())
            for g_ in range(4):
                B.em("vector", lambda v, gg=g_: v.tensor_single_scalar(
                    r_sel[:, gg * 6:(gg + 1) * 6], rt_bt[:, gg * 6:(gg + 1) * 6],
                    r_m2[:, gg:gg + 1], ALU.is_ge))
            B.em("vector", lambda v: v.drain())
            B.em("vector", lambda v: v.tensor_mul(r_gs[:, :], rt_gt[:, :], r_sel[:, :]))
            B.em("vector", lambda v: v.drain())
            B.em("vector", lambda v: v.reduce_sum(vo1(r_S.ap()), v3(r_gs.ap(), 4), AX))
            B.em("vector", lambda v: v.drain())
            for g_ in range(4):
                B.em("vector", lambda v, gg=g_: v.tensor_scalar(
                    r_d[:, gg * 6:(gg + 1) * 6], rt_gt[:, gg * 6:(gg + 1) * 6],
                    2.0, r_S[:, gg:gg + 1], ALU.mult, ALU.subtract),
                    incs=([(V, 1)] if g_ == 3 else []))
            B.em("scalar", lambda s: s.activation(
                r_wp[:, :], r_d[:, :], AF.Sigmoid, bias=0.0, scale=1.0),
                waits=[cap(V)], incs=[(SS, 1)])
            B.em("vector", lambda v: v.tensor_mul(r_w[:, :], r_wp[:, :], r_sel[:, :]),
                 waits=[cap(SS)], incs=[(V, 1)])
            return V.n

        def router_wtrans(w_v):
            bw, bwe = banks.acq()
            for g_ in range(4):
                B.em("tensor", lambda t, b=bw, gg=g_: t.transpose(
                    bankap(b, (slice(0, 6), slice(gg * 128, (gg + 1) * 128))),
                    r_w[:, gg * 6:(gg + 1) * 6], idt[:, :]),
                    waits=(bwe if g_ == 0 else []) + [(V, w_v)],
                    incs=([(T, 1)] if g_ == 3 else []))
            B.em("vector", lambda v, b=bw: v.tensor_copy(
                w6b[:, :], bankap(b, (slice(0, 6), slice(0, 512)))),
                waits=[cap(T)], incs=[(V, 1)])
            banks.release(bw, cap(V))
            return V.n

        def phase_up(use0, use1, bias_tile, bias_off, xnb_sig, par):
            uses = [use0, use1]
            hoff = par * 4096
            for oc in range(FC):
                half = oc // 4
                i, pos, wl = uses[half]
                bi, be = banks.acq()
                for kc in range(FC):
                    incs = [(T, 1)] if kc == 7 else []
                    B.em("tensor", lambda t, b=bi, k=kc, p=pos, tt=oc % 4: t.matmul(
                        bankap(b), wslice(p, k, tt),
                        xnb[:, k * 512:(k + 1) * 512], start=(k == 0), stop=(k == 7)),
                        waits=(be if kc == 0 else []) +
                              ([wl] if kc == 0 else []) + [xnb_sig[kc]],
                        incs=incs)
                if oc % 4 == 3:
                    ring.freed(i, full_order)
                B.em("scalar", lambda s, b=bi, o=oc, bo=bias_off + oc, bt=bias_tile:
                     s.activation(hb2[:, hoff + o * 512: hoff + o * 512 + 512],
                                  bankap(b), AF.Identity,
                                  bias=bt[:, bo:bo + 1], scale=1.0),
                     waits=[cap(T), cap(G), (V, qg["V%d" % par])], incs=[(SS, 1)])
                banks.release(bi, cap(SS))
            return SS.n

        def phase_gate_prod(useg0, useg1, bias_off, wrow_bank, shared, h_ss, par,
                            free_ring=True, sum_into_prod=False):
            uses = [useg0, useg1]
            hoff = par * 4096
            for oc in range(FC):
                half = oc // 4
                i, pos, wl = uses[half]
                bi, be = banks.acq()
                for kc in range(FC):
                    incs = [(T, 1)] if kc == 7 else []
                    B.em("tensor", lambda t, b=bi, k=kc, p=pos, tt=oc % 4: t.matmul(
                        bankap(b), wslice(p, k, tt),
                        hb2[:, hoff + k * 512: hoff + (k + 1) * 512],
                        start=(k == 0), stop=(k == 7)),
                        waits=(be if kc == 0 else []) + ([wl] if kc == 0 else []) +
                              ([(SS, h_ss)] if kc == 0 else []),
                        incs=incs)
                if oc % 4 == 3 and free_ring:
                    ring.freed(i, full_order)
                sl = oc % 2
                B.em("scalar", lambda s, b=bi, ss=sl, bo=bias_off + oc,
                     bt=(bsg if shared else bgt_):
                     s.activation(sgt[:, ss * 512:(ss + 1) * 512], bankap(b), AF.Silu,
                                  bias=bt[:, bo:bo + 1], scale=1.0),
                     waits=[cap(T), cap(G)] + ([cap(V)] if shared else []),
                     incs=[(SS, 1)])
                banks.release(bi, cap(SS))
                if shared and sum_into_prod:
                    B.em("vector", lambda v, ss=sl, o=oc: v.tensor_tensor(
                        hb2[:, o * 512:(o + 1) * 512],
                        sgt[:, ss * 512:(ss + 1) * 512],
                        hb2[:, hoff + o * 512: hoff + (o + 1) * 512], ALU.mult),
                        waits=[cap(SS), cap(T), (V, qg["V0"])], incs=[(V, 1)])
                    B.em("vector", lambda v, o=oc: v.tensor_tensor(
                        prod[:, o * 512:(o + 1) * 512],
                        prod[:, o * 512:(o + 1) * 512],
                        hb2[:, o * 512:(o + 1) * 512], ALU.add),
                        waits=[(V, V.n)], incs=[(V, 1)])
                elif shared:
                    B.em("vector", lambda v, ss=sl, o=oc: v.tensor_tensor(
                        prod[:, o * 512:(o + 1) * 512],
                        sgt[:, ss * 512:(ss + 1) * 512],
                        hb2[:, hoff + o * 512: hoff + (o + 1) * 512], ALU.mult),
                        waits=[cap(SS), cap(T)], incs=[(V, 1)])
                else:
                    B.em("gpsimd", lambda g, ss=sl, o=oc: g.tensor_tensor(
                        p1t[:, ss * 512:(ss + 1) * 512],
                        sgt[:, ss * 512:(ss + 1) * 512],
                        hb2[:, hoff + o * 512: hoff + (o + 1) * 512], ALU.mult),
                        waits=[cap(SS), cap(V)], incs=[(G, 1)])
                    wb = bankap(wrow_bank).rearrange("p (o t) -> p o t", o=1)
                    B.em("vector", lambda v, ss=sl, o=oc, w=wb: v.tensor_tensor(
                        prod.ap()[:, o * 512:(o + 1) * 512].rearrange(
                            "p (o t) -> p o t", o=1),
                        p1t.ap()[:, ss * 512:(ss + 1) * 512].rearrange(
                            "p (o t) -> p o t", o=1),
                        w, ALU.mult),
                        waits=[cap(G), cap(T)], incs=[(V, 1)])
            if not shared:
                banks.release(wrow_bank, cap(V))
            return V.n

        def phase_down(used0, used1, first_slot, with_bdn, w6b_v, prod_v, xf_sig,
                       yac_sig_out, free_ring=True):
            uses = [used0, used1]
            B.em("vector", lambda v: v.drain())
            for m in range(FC):
                half = m // 4
                i, pos, wl = uses[half]
                bi, be = banks.acq()
                for kc in range(FC):
                    incs = [(T, 1)] if (kc == 7 and not with_bdn) else []
                    B.em("tensor", lambda t, b=bi, k=kc, p=pos, tt=m % 4: t.matmul(
                        bankap(b), wslice(p, k, tt),
                        prod[:, k * 512:(k + 1) * 512],
                        start=(k == 0), stop=(k == 7 and not with_bdn)),
                        waits=(be if kc == 0 else []) + ([wl] if kc == 0 else []) +
                              ([(V, prod_v)] if kc == 0 else []),
                        incs=incs)
                if with_bdn:
                    B.em("tensor", lambda t, b=bi, mm=m: t.matmul(
                        bankap(b), bdn[:, mm * 128:(mm + 1) * 128],
                        w6b[:, :], start=False, stop=True),
                        waits=[(V, w6b_v)], incs=[(T, 1)])
                if m % 4 == 3 and free_ring:
                    ring.freed(i, full_order)
                if first_slot:
                    B.em("vector", lambda v, b=bi, mm=m: v.scalar_tensor_tensor(
                        yac[:, mm * 512:(mm + 1) * 512], bankap(b),
                        csd[:, mm:mm + 1], xf[:, mm * 512:(mm + 1) * 512],
                        ALU.add, ALU.add),
                        waits=[cap(T), xf_sig[m]], incs=[(V, 1)])
                else:
                    B.em("vector", lambda v, b=bi, mm=m: v.tensor_tensor(
                        yac[:, mm * 512:(mm + 1) * 512], yac[:, mm * 512:(mm + 1) * 512],
                        bankap(b), ALU.add),
                        waits=[cap(T)], incs=[(V, 1)])
                if yac_sig_out is not None:
                    yac_sig_out[m] = (V, V.n)
                banks.release(bi, cap(V))

        def ffn_iter(j, xf_sig, xnb_sig, fillers=None):
            rt_v = router_mm(j, xf_sig)
            u0 = [ring.use(full_order, s) for s in blk_up(0)]
            h_ss = {0: phase_up(u0[0], u0[1], bup, 0, xnb_sig, 0)}
            w_v = router_transposes(rt_v)
            u1 = [ring.use(full_order, s) for s in blk_up(1)]
            h_ss[1] = phase_up(u1[0], u1[1], bup, 8, xnb_sig, 1)
            w6b_v = router_wtrans(w_v)
            yac_sig = [None] * FC
            for k in range(8):
                if fillers and k in (1, 3, 5, 7):
                    fillers.pop(0)()
                shared = k >= E
                if not shared:
                    bw, bwe = banks.acq()
                    B.em("tensor", lambda t, b=bw, ee=k: t.matmul(
                        bankap(b), oh6[:, ee * 128:(ee + 1) * 128], w6b[0:6, :],
                        start=True, stop=True),
                        waits=bwe + [(V, w6b_v)], incs=[(T, 1)])
                else:
                    bw = None
                if k == 7:
                    g_ = hold_sg
                else:
                    g_ = [ring.use(full_order, s) for s in blk_gate(k)]
                if k == 6:
                    hold_sg = g_
                prod_v = phase_gate_prod(
                    g_[0], g_[1], (k * 8 if not shared else 0), bw, shared,
                    h_ss[k], k % 2, free_ring=(k != 6),
                    sum_into_prod=(k == 7))
                if shared:
                    qg["V0"] = V.n
                    if k == 7:
                        qg["V1"] = V.n
                if k + 2 < 8:
                    nu = [ring.use(full_order, s) for s in blk_up(k + 2)]
                if k == 7:
                    d_ = hold_sd
                else:
                    d_ = [ring.use(full_order, s) for s in blk_down(k)]
                if k == 6:
                    hold_sd = d_
                if k + 2 < 8:
                    kk = k + 2
                    bias_t, bias_o = (bup, kk * 8) if kk < E else (bsup, (kk - E) * 8)
                    h_ss[kk] = phase_up(nu[0], nu[1], bias_t, bias_o, xnb_sig, kk % 2)
                if k != 6:
                    phase_down(d_[0], d_[1], k == 0, k == 7, w6b_v, prod_v,
                               xf_sig, yac_sig, free_ring=True)
            return rms(j, yac_sig)

        def at(par, idx):
            off = par * 2048 + idx * 512
            return yac.ap()[:, off:off + 512]

        atrk = {"scan_v": [0, 0], "b_g": [0, 0], "carry_ss": [0] * FC}

        def attn(xsrc, xdst, xnb_sig):
            uz = [ring.use(full_order, SLOT_WZ)]
            uh = [ring.use(full_order, SLOT_WH)]
            uz.append(ring.use(full_order, SLOT_WZ + 1))
            uh.append(ring.use(full_order, SLOT_WH + 1))
            x2_sig = []
            for fc in range(FC):
                par = fc % 2
                half = fc // 4
                iz, pz, wlz = uz[half]
                ih, ph, wlh = uh[half]
                bk, bke = banks.acq()
                bt, bte = banks.acq()
                for kc in range(FC):
                    incs = [(T, 1)] if kc == 7 else []
                    B.em("tensor", lambda t, b=bk, k=kc, p=pz, tt=fc % 4: t.matmul(
                        bankap(b), wslice(p, k, tt),
                        xnb[:, k * 512:(k + 1) * 512], start=(k == 0), stop=(k == 7)),
                        waits=(bke if kc == 0 else []) + ([wlz] if kc == 0 else []) +
                              [xnb_sig[kc]],
                        incs=incs)
                if fc % 4 == 3:
                    ring.freed(iz, full_order)
                for kc in range(FC):
                    incs = [(T, 1)] if kc == 7 else []
                    B.em("tensor", lambda t, b=bt, k=kc, p=ph, tt=fc % 4: t.matmul(
                        bankap(b), wslice(p, k, tt),
                        xnb[:, k * 512:(k + 1) * 512], start=(k == 0), stop=(k == 7)),
                        waits=(bte if kc == 0 else []) + ([wlh] if kc == 0 else []),
                        incs=incs)
                if fc % 4 == 3:
                    ring.freed(ih, full_order)
                kp = T.n
                pw = [(T, kp), (V, atrk["scan_v"][par]), (G, atrk["b_g"][par])]
                B.em("scalar", lambda s, b=bk, f=fc, p=par: s.activation(
                    at(p, 0), bankap(b), AF.Sigmoid, bias=nbz[:, f:f + 1], scale=-1.0),
                    waits=pw, incs=[(SS, 1)])
                a_ss = SS.n
                B.em("scalar", lambda s, b=bt, f=fc, p=par: s.activation(
                    at(p, 2), bankap(b), AF.Sigmoid, bias=bh[:, f:f + 1], scale=1.0))
                B.em("scalar", lambda s, b=bt, f=fc, p=par: s.activation(
                    at(p, 3), bankap(b), AF.Relu, bias=bh[:, f:f + 1], scale=1.0),
                    incs=[(SS, 1)])
                r_ss = SS.n
                banks.release(bk, (SS, a_ss))
                banks.release(bt, (SS, r_ss))
                B.em("vector", lambda v, p=par: v.tensor_scalar(
                    at(p, 1), at(p, 0), -1.0, 1.0, ALU.mult, ALU.add),
                    waits=[(SS, a_ss), (G, atrk["b_g"][par])])
                B.em("vector", lambda v, p=par: v.scalar_tensor_tensor(
                    at(p, 2), at(p, 2), 0.5, at(p, 3), ALU.min, ALU.add),
                    waits=[(SS, r_ss)], incs=[(V, 1)])
                gt_v = V.n
                B.em("gpsimd", lambda g, p=par: g.tensor_mul(
                    at(p, 3), at(p, 1), at(p, 2)),
                    waits=[(V, gt_v)], incs=[(G, 1)])
                atrk["b_g"][par] = G.n
                b_g = G.n
                B.em("vector", lambda v: v.drain())
                for s_ in range(SEQ):
                    B.em("vector", lambda v, f=fc, ss=s_, p=par: v.tensor_tensor_scan(
                        xf[:, f * 512 + ss * 128: f * 512 + (ss + 1) * 128],
                        at(p, 0)[:, ss * 128:(ss + 1) * 128],
                        at(p, 3)[:, ss * 128:(ss + 1) * 128],
                        carry[:, f * 4 + ss: f * 4 + ss + 1],
                        ALU.mult, ALU.add),
                        waits=([(G, b_g), (SS, atrk["carry_ss"][fc])]
                               if s_ == 0 else []),
                        incs=([(V, 1)] if s_ == SEQ - 1 else []))
                atrk["scan_v"][par] = V.n
                sc_v = V.n
                B.em("scalar", lambda s, f=fc: s.activation(
                    carry[:, f * 4: f * 4 + 4],
                    xf.ap()[:, f * 512:(f + 1) * 512].rearrange(
                        "p (s t) -> p s t", s=4)[:, :, 127],
                    AF.Identity, bias=0.0, scale=1.0),
                    waits=[(V, sc_v)], incs=[(SS, 1)])
                atrk["carry_ss"][fc] = SS.n
                B.em("gpsimd", lambda g, f=fc, xs=xsrc, xd=xdst: g.tensor_add(
                    xd[:, f * 512:(f + 1) * 512], xs[:, f * 512:(f + 1) * 512],
                    xf[:, f * 512:(f + 1) * 512]),
                    waits=[(V, sc_v)], incs=[(G, 1)])
                x2_sig.append((G, G.n))
            return x2_sig

        state = {"in_tg": 0}

        def emit_in_dmas(ci):
            t0 = ci * TLOC
            for pp_ in range(2):
                B.em("gpsimd", lambda g, tt=t0, pp=pp_: g.dma_start(
                    out=stage.ap()[:, pp * 2048:(pp + 1) * 2048].rearrange(
                        "p (s d) -> p s d", s=2),
                    in_=x_in[2 * pp:2 * pp + 2, tt:tt + TLOC, :].rearrange(
                        "s t d -> t s d")),
                    waits=[(T, state["in_tg"])], incs=[(XL, 16)])
            return XL.n

        def emit_in_passes(xl_v, x3_guard):
            x0_sig = [None] * FC
            for dc in range(FC):
                vn = None
                for s_ in range(SEQ):
                    bi, be = banks.acq()
                    B.em("tensor", lambda t, b=bi, ss=s_, d=dc: t.transpose(
                        bankap(b, (slice(0, 128), slice(0, 128))),
                        stage[:, ss * 1024 + d * 128: ss * 1024 + (d + 1) * 128],
                        idt[:, :]),
                        waits=be + [(XL, xl_v)],
                        incs=[(T, 1)])
                    guard = [x3_guard[dc]] if x3_guard is not None else []
                    if s_ % 2 == 0:
                        B.em("vector", lambda v, b=bi, ss=s_, d=dc: v.tensor_copy(
                            xA[:, d * 512 + ss * 128: d * 512 + (ss + 1) * 128],
                            bankap(b, (slice(0, 128), slice(0, 128)))),
                            waits=[cap(T)] + guard, incs=[(V, 1)])
                        vn = (V, V.n)
                        banks.release(bi, cap(V))
                    else:
                        B.em("scalar", lambda s, b=bi, ss=s_, d=dc: s.activation(
                            xA[:, d * 512 + ss * 128: d * 512 + (ss + 1) * 128],
                            bankap(b, (slice(0, 128), slice(0, 128))),
                            AF.Identity, bias=0.0, scale=1.0),
                            waits=[cap(T)] + guard, incs=[(SS, 1)])
                        banks.release(bi, cap(SS))
                x0_sig[dc] = [vn, (SS, SS.n)]
            state["in_tg"] = T.n
            return x0_sig

        def emit_out_group(ci_prev, tt, ph, x3s):
            ro = 4096 + (ph % 2) * 1024
            ogs = OGs[ph % 2]
            og_need = 16 * (2 * ci_prev + (ph // 2))
            for dc in range(FC):
                bi, be = banks.acq()
                B.em("tensor", lambda t, b=bi, ss=ph, d=dc: t.transpose(
                    bankap(b, (slice(0, 128), slice(0, 128))),
                    xB[:, d * 512 + ss * 128: d * 512 + (ss + 1) * 128],
                    idt[:, :]),
                    waits=be + [x3s[dc]],
                    incs=[(T, 1)])
                if dc % 2 == 0:
                    B.em("vector", lambda v, b=bi, d=dc: v.tensor_copy(
                        stage[:, ro + d * 128: ro + (d + 1) * 128],
                        bankap(b, (slice(0, 128), slice(0, 128)))),
                        waits=[cap(T), (ogs, og_need)],
                        incs=[(V, 1)])
                    banks.release(bi, cap(V))
                else:
                    B.em("scalar", lambda s, b=bi, d=dc: s.activation(
                        stage[:, ro + d * 128: ro + (d + 1) * 128],
                        bankap(b, (slice(0, 128), slice(0, 128))),
                        AF.Identity, bias=0.0, scale=1.0),
                        waits=[cap(T), (ogs, og_need)],
                        incs=[(SS, 1)])
                    banks.release(bi, cap(SS))
            B.em("gpsimd", lambda g, tt=tt, pp=ph, r=ro: g.dma_start(
                out=y_out[pp:pp + 1, tt:tt + TLOC, :].rearrange(
                    "s t d -> t s d"),
                in_=stage.ap()[:, r:r + 1024].rearrange("p (s d) -> p s d", s=1)),
                waits=[(V, V.n), (SS, SS.n)], incs=[(ogs, 16)])

        pending_in = None
        x3_prev = None
        prev_out = None
        for ci in range(n_chunks):
            t0 = ci * TLOC
            if pending_in is None:
                B.em("vector", lambda v: v.drain())
                xl_v = emit_in_dmas(ci)
                x0_sig = emit_in_passes(xl_v, None)
            else:
                x0_sig = emit_in_passes(pending_in, x3_prev)
            if prev_out is not None:
                p_ci, p_t0, p_x3 = prev_out
                for ph in range(4):
                    emit_out_group(p_ci, p_t0, ph, p_x3)
            xf_sig, xnb_sig = layernorm(xA, x0_sig, LN_EPS)
            xf_sig, xnb_sig = ffn_iter(0, xf_sig, xnb_sig)
            xf_sig, xnb_sig = ffn_iter(1, xf_sig, xnb_sig)
            B.em("vector", lambda v: v.drain())
            x1_sig = []
            for fc in range(FC):
                B.em("vector", lambda v, f=fc: v.scalar_tensor_tensor(
                    xB[:, f * 512:(f + 1) * 512], xf[:, f * 512:(f + 1) * 512],
                    0.5, xA[:, f * 512:(f + 1) * 512], ALU.mult, ALU.add),
                    waits=[xf_sig[fc], cap(SS)], incs=[(V, 1)])
                x1_sig.append((V, V.n))
            xf_sig, xnb_sig = layernorm(xB, x1_sig, LN_EPS)
            x2_sig = attn(xB, xA, xnb_sig)
            xf_sig, xnb_sig = layernorm(xA, x2_sig, LN_EPS)
            xf_sig, xnb_sig = ffn_iter(0, xf_sig, xnb_sig)
            xf_sig, xnb_sig = ffn_iter(1, xf_sig, xnb_sig)
            B.em("vector", lambda v: v.drain())
            x3_sig = []
            for fc in range(FC):
                B.em("vector", lambda v, f=fc: v.scalar_tensor_tensor(
                    xB[:, f * 512:(f + 1) * 512], xf[:, f * 512:(f + 1) * 512],
                    0.5, xA[:, f * 512:(f + 1) * 512], ALU.mult, ALU.add),
                    waits=[xf_sig[fc], cap(SS)], incs=[(V, 1)])
                x3_sig.append((V, V.n))
            if ci + 1 < n_chunks:
                pending_in = emit_in_dmas(ci + 1)
                x3_prev = x3_sig
            prev_out = (ci, t0, x3_sig)

        p_ci, p_t0, p_x3 = prev_out
        for ph in range(4):
            emit_out_group(p_ci, p_t0, ph, p_x3)

        B.em("gpsimd", lambda g: g.nop(),
             waits=[(OGs[0], 32 * n_chunks), (OGs[1], 32 * n_chunks)])

        assert ring.use_i == len(full_order), (ring.use_i, len(full_order))

        with nc.Block() as block:
            B.replay(block)
    return nc


def kernel(**inputs):
    nc = build_nc(NCHUNK_FULL)
    in_maps = [host_inputs(inputs, c) for c in range(8)]
    res = run_bass_kernel_spmd(nc, in_maps, core_ids=list(range(8)))
    out = np.concatenate([r["y_out"] for r in res.results], axis=0)
    return out.astype(np.float32)


# version 20 (final)
# speedup vs baseline: 1.1706x; 1.0390x over previous
import contextlib
import numpy as np

import concourse.bass as bass
import concourse.mybir as mybir
from concourse.bass_utils import run_bass_kernel_spmd

F32 = mybir.dt.float32
BF16 = mybir.dt.bfloat16
AF = mybir.ActivationFunctionType
ALU = mybir.AluOpType
AX = mybir.AxisListType.X

D = 1024
E = 6
FC = 8
NT = 512
SEQ = 4
TLOC = 128
NCHUNK_FULL = 16
RING = 6
LN_EPS = 1e-5
RMS_EPS = 1e-6

SLOT_SUP = 36
SLOT_SG = 40
SLOT_SD = 42
SLOT_WZ = 44
SLOT_WH = 46
NSLOT = 48


def _halves(W):
    W = np.asarray(W, dtype=np.float32)
    K, M = W.shape
    assert K == 1024 and M % 512 == 0
    out = []
    for c in range(0, M, 512):
        H = W[:, c:c + 512].reshape(8, 128, 512).transpose(1, 0, 2).reshape(128, 4096)
        out.append(np.ascontiguousarray(H))
    return out


def _pcol(v):
    return np.ascontiguousarray(np.asarray(v, np.float32).reshape(8, 128).T)


def host_inputs(ins, core_id):
    x = np.ascontiguousarray(ins["L0_attn_h_prev_layer"][4 * core_id:4 * core_id + 4])

    halves = []
    for e in range(E):
        halves += _halves(ins["L0_ffn_exp_Wup"][e])
        halves += _halves(ins["L0_ffn_exp_Wg"][e])
        halves += _halves(ins["L0_ffn_exp_Wdown"][e])
    halves += _halves(ins["L0_ffn_share_Wup"])
    halves += _halves(ins["L0_ffn_share_Wg"])
    halves += _halves(ins["L0_ffn_share_Wdown"])
    halves += _halves(np.asarray(ins["L0_attn_W_z"], np.float32).T)
    halves += _halves(np.asarray(ins["L0_attn_W_h"], np.float32).T)
    wblob = np.stack(halves)

    bup = np.asarray(ins["L0_ffn_exp_bup"], np.float32).reshape(E, 8, 128)
    bup = np.ascontiguousarray(bup.transpose(2, 0, 1).reshape(128, E * 8))
    bg = np.asarray(ins["L0_ffn_exp_bg"], np.float32).reshape(E, 8, 128)
    bg = np.ascontiguousarray(bg.transpose(2, 0, 1).reshape(128, E * 8))
    nw = np.asarray(ins["L0_ffn_norm_w"], np.float32).reshape(2, 8, 128)
    nw = np.ascontiguousarray(nw.transpose(2, 0, 1).reshape(128, 16))
    rw = np.asarray(ins["L0_ffn_router_W"], np.float32).reshape(2, 8, 128, E)
    rw = np.ascontiguousarray(rw.transpose(2, 0, 1, 3).reshape(128, 2 * 8 * E))

    return {
        "x_in": x,
        "wblob": wblob,
        "t_bz": _pcol(ins["L0_attn_b_z"]),
        "t_bh": _pcol(ins["L0_attn_b_h"]),
        "t_bup": bup,
        "t_bg": bg,
        "t_bsup": np.ascontiguousarray(
            np.asarray(ins["L0_ffn_share_bup"], np.float32).reshape(16, 128).T),
        "t_bsg": _pcol(ins["L0_ffn_share_bg"]),
        "t_bsd": _pcol(ins["L0_ffn_share_bdown"]),
        "t_nw": nw,
        "t_rw": rw,
        "t_rb": np.ascontiguousarray(np.asarray(ins["L0_ffn_router_b"], np.float32).T),
        "t_gb": np.ascontiguousarray(np.asarray(ins["L0_ffn_gate_bias"], np.float32).T),
        "t_bdn": np.ascontiguousarray(np.asarray(ins["L0_ffn_exp_bdown"], np.float32)),
        "t_id": np.eye(128, dtype=np.float32),
        "t_oh": np.ascontiguousarray(
            np.repeat(np.eye(6, dtype=np.float32), 128, axis=1)),
    }


class Sem:
    def __init__(self, h):
        self.h = h
        self.n = 0


class Builder:
    ENGS = ("sync", "tensor", "vector", "scalar", "gpsimd")

    def __init__(self):
        self.ops = {e: [] for e in self.ENGS}
        self.last_wait = {e: {} for e in self.ENGS}

    def em(self, eng, fn, waits=(), incs=()):
        lw = self.last_wait[eng]
        ws = []
        for s, v in waits:
            if v <= 0:
                continue
            if lw.get(id(s), -1) >= v:
                continue
            lw[id(s)] = v
            ws.append((s, v))
        merged = {}
        for s, a in incs:
            merged[id(s)] = (s, merged.get(id(s), (s, 0))[1] + a)
        incs = list(merged.values())
        assert len(incs) <= 1, f"multiple sems inc'd by one op: {incs}"
        for s, a in incs:
            s.n += a
        self.ops[eng].append((tuple(ws), fn, tuple(incs)))

    def replay(self, block):
        for eng in self.ENGS:
            dec = getattr(block, eng)
            ops = self.ops[eng]

            def make(ops):
                def _body(engobj):
                    for ws, fn, incs in ops:
                        for s, v in ws:
                            engobj.wait_ge(s.h, v)
                        ins = fn(engobj)
                        for s, a in incs:
                            ins = ins.then_inc(s.h, a)
                return _body
            dec(make(ops))


def cap(sem):
    return (sem, sem.n)


class Banks:

    def __init__(self, tiles):
        self.t = tiles
        self.state = [[] for _ in tiles]
        self.rr = 0

    def acq(self):
        n = len(self.t)
        for k in range(n):
            i = (self.rr + k) % n
            if self.state[i] != "inuse":
                edges = list(self.state[i])
                self.state[i] = "inuse"
                self.rr = i + 1
                return i, edges
        raise RuntimeError("all PSUM banks in use")

    def release(self, i, edge):
        if self.state[i] == "inuse":
            self.state[i] = [edge]
        else:
            self.state[i].append(edge)


def build_nc(n_chunks=NCHUNK_FULL):
    nc = bass.Bass("TRN2", target_bir_lowering=False, debug=False)

    x_in = nc.dram_tensor("x_in", [SEQ, 2048, D], F32, kind="ExternalInput").ap()
    wblob = nc.dram_tensor("wblob", [NSLOT, 128, 4096], F32, kind="ExternalInput").ap()
    names_small = ["t_bz", "t_bh", "t_bup", "t_bg", "t_bsup", "t_bsg", "t_bsd",
                   "t_nw", "t_rw", "t_rb", "t_gb", "t_bdn", "t_id", "t_oh"]
    shapes_small = {"t_bz": [128, 8], "t_bh": [128, 8], "t_bup": [128, 48],
                    "t_bg": [128, 48], "t_bsup": [128, 16], "t_bsg": [128, 8],
                    "t_bsd": [128, 8], "t_nw": [128, 16], "t_rw": [128, 96],
                    "t_rb": [6, 2], "t_gb": [6, 2], "t_bdn": [6, 1024],
                    "t_id": [128, 128], "t_oh": [6, 768]}
    dsmall = {n: nc.dram_tensor(n, shapes_small[n], F32, kind="ExternalInput").ap()
              for n in names_small}
    y_out = nc.dram_tensor("y_out", [SEQ, 2048, D], F32, kind="ExternalOutput").ap()
    wscr = nc.dram_tensor("wscr", [NSLOT, 128, 4096], BF16, kind="Internal").ap()

    ctx = contextlib.ExitStack()
    with ctx:
        def sb(name, shape, dt):
            return ctx.enter_context(nc.sbuf_tensor(name, shape, dt))

        wring = sb("wring", [128, RING * 4096], BF16)
        xA = sb("xA", [128, 4096], F32)
        xB = sb("xB", [128, 4096], F32)
        xf = sb("xf", [128, 4096], F32)
        xnb = sb("xnb", [128, 4096], BF16)
        hb2 = sb("hb2", [128, 8192], BF16)
        yac = sb("yac", [128, 4096], F32)
        prod = sb("prod", [128, 4096], BF16)
        sgt = sb("sgt", [128, 1024], F32)
        p1t = sb("p1t", [128, 1024], F32)
        tnt = sb("tnt", [128, 1024], F32)
        stage = sb("stage", [128, 6144], F32)
        rowA = sb("rowA", [1, 512], F32)
        rowB = sb("rowB", [1, 512], F32)
        Rs = sb("Rs", [128, 512], F32)
        MRs = sb("MRs", [128, 512], F32)
        rt_b6 = sb("rt_b6", [6, 512], F32)
        rt_g6 = sb("rt_g6", [6, 512], F32)
        w6b = sb("w6b", [6, 512], BF16)
        rt_bt = sb("rt_bt", [128, 24], F32)
        rt_gt = sb("rt_gt", [128, 24], F32)
        r_eq = sb("r_eq", [128, 24], F32)
        r_b2 = sb("r_b2", [128, 24], F32)
        r_sel = sb("r_sel", [128, 24], F32)
        r_gs = sb("r_gs", [128, 24], F32)
        r_d = sb("r_d", [128, 24], F32)
        r_wp = sb("r_wp", [128, 24], F32)
        r_w = sb("r_w", [128, 24], F32)
        r_m1 = sb("r_m1", [128, 4], F32)
        r_m2 = sb("r_m2", [128, 4], F32)
        r_S = sb("r_S", [128, 4], F32)
        carry = sb("carry", [128, 32], F32)
        bz = sb("bz", [128, 8], F32)
        nbz = sb("nbz", [128, 8], F32)
        bh = sb("bh", [128, 8], F32)
        bup = sb("bup", [128, 48], F32)
        bgt_ = sb("bgt", [128, 48], F32)
        bsup = sb("bsup", [128, 16], F32)
        bsg = sb("bsg", [128, 8], F32)
        bsd = sb("bsd", [128, 8], F32)
        csd = sb("csd", [128, 8], F32)
        nw = sb("nw", [128, 16], F32)
        rw = sb("rw", [128, 96], F32)
        rb = sb("rb", [6, 2], F32)
        gb = sb("gb", [6, 2], F32)
        bdn = sb("bdn", [6, 1024], BF16)
        oh6 = sb("oh6", [6, 768], BF16)
        idt = sb("idt", [128, 128], F32)
        c_eps_ln = sb("c_eps_ln", [1, 1], F32)
        c_eps_rms = sb("c_eps_rms", [1, 1], F32)
        nc.const_aps.register(F32, LN_EPS, c_eps_ln.ap())
        nc.const_aps.register(F32, RMS_EPS, c_eps_rms.ap())
        nwrow8 = sb("nwrow8", [1, 1024], F32)
        ones_bc = sb("ones_bc", [128, 1], BF16)
        ones_fr = sb("ones_fr", [1, 128], F32)

        bank_t = [ctx.enter_context(nc.psum_tensor(f"pb{i}", [128, 512], F32))
                  for i in range(8)]

        S = {n: Sem(ctx.enter_context(nc.semaphore(n))) for n in
             ("BL", "XL", "OG", "OH", "T", "V", "SS", "G")}
        BL, XL, OG = S["BL"], S["XL"], S["OG"]
        OGs = [S["OG"], S["OH"]]
        T, V, SS, G = S["T"], S["V"], S["SS"], S["G"]
        WLp = [Sem(ctx.enter_context(nc.semaphore(f"WL{i}"))) for i in range(RING)]
        CVg = [Sem(ctx.enter_context(nc.semaphore(f"CV{i}"))) for i in range(6)]
        CV_GROUPS = [2, 4, 6, 12, 12, 12]
        CV_OF = []
        for gi, n_ in enumerate(CV_GROUPS):
            CV_OF += [gi] * n_

        B = Builder()
        banks = Banks(bank_t)

        def bankap(i, sl=None):
            return bank_t[i][:, :] if sl is None else bank_t[i][sl[0], sl[1]]

        sb_map = {"t_bz": bz, "t_bh": bh, "t_bup": bup, "t_bg": bgt_,
                  "t_bsup": bsup, "t_bsg": bsg, "t_bsd": bsd, "t_nw": nw,
                  "t_rw": rw, "t_rb": rb, "t_gb": gb, "t_id": idt}
        for n, tile in sb_map.items():
            B.em("gpsimd", lambda g, t=tile, a=dsmall[n]: g.dma_start(out=t[:, :], in_=a),
                 incs=[(BL, 16)])
        B.em("gpsimd", lambda g: g.dma_start(out=bdn[:, :], in_=dsmall["t_bdn"]),
             incs=[(BL, 16)])
        B.em("gpsimd", lambda g: g.dma_start(out=oh6[:, :], in_=dsmall["t_oh"]),
             incs=[(BL, 16)])
        bl_total = BL.n

        for s_ in range(NSLOT):
            B.em("gpsimd", lambda g, s=s_: g.dma_start(out=wscr[s], in_=wblob[s]),
                 incs=[(CVg[CV_OF[s_]], 16)])
        conv_done = [(CVg[CV_OF[s]], 16 * CV_GROUPS[CV_OF[s]]) for s in range(NSLOT)]

        B.em("vector", lambda v: v.memset(ones_bc[:, :], 1.0), waits=[cap(BL)])
        B.em("vector", lambda v: v.memset(c_eps_ln[:, :], LN_EPS))
        B.em("vector", lambda v: v.memset(c_eps_rms[:, :], RMS_EPS))
        B.em("vector", lambda v: v.memset(ones_fr[:, :], 1.0))
        B.em("vector", lambda v: v.memset(carry[:, :], 0.0))
        B.em("vector", lambda v: v.tensor_scalar_mul(nbz[:, :], bz[:, :], -1.0))
        B.em("vector", lambda v: v.tensor_scalar_mul(csd[:, :], bsd[:, :], 2.0),
             incs=[(V, 1)])
        pre_v = V.n

        B.em("tensor", lambda t: t.nop(), waits=[(BL, bl_total), (V, pre_v)])
        B.em("scalar", lambda s: s.nop(), waits=[(BL, bl_total), (V, pre_v)])
        B.em("gpsimd", lambda g: g.nop(), waits=[(BL, bl_total)])

        class Ring:
            def __init__(self):
                self.load_i = 0
                self.use_i = 0
                self.load_wl = []
                self.free_wf = []

            def pump(self, order):
                while (self.load_i < min(len(order), self.use_i + RING)
                       and (self.load_i < RING
                            or self.free_wf[self.load_i - RING] is not None)):
                    li = self.load_i
                    slot = order[li]
                    pos = li % RING
                    waits = []
                    if li >= RING:
                        waits.append((T, self.free_wf[li - RING]))
                    waits.append(conv_done[slot])
                    B.em("sync",
                         lambda s, p=pos, sl=slot: s.dma_start(
                             out=wring[:, p * 4096:(p + 1) * 4096], in_=wscr[sl]),
                         waits=waits, incs=[(WLp[pos], 16)])
                    self.load_wl.append((WLp[pos], WLp[pos].n))
                    self.free_wf.append(None)
                    self.load_i += 1

            def use(self, order, expect_slot):
                i = self.use_i
                assert order[i] == expect_slot, (i, order[i], expect_slot)
                self.pump(order)
                self.use_i += 1
                pos = i % RING
                return i, pos, self.load_wl[i]

            def freed(self, i, order):
                self.free_wf[i] = T.n
                self.pump(order)

        ring = Ring()

        def blk_up(k):
            if k < E:
                return [6 * k, 6 * k + 1]
            return ([SLOT_SUP, SLOT_SUP + 1] if k == E
                    else [SLOT_SUP + 2, SLOT_SUP + 3])

        def blk_gate(k):
            return [6 * k + 2, 6 * k + 3] if k < E else [SLOT_SG, SLOT_SG + 1]

        def blk_down(k):
            return [6 * k + 4, 6 * k + 5] if k < E else [SLOT_SD, SLOT_SD + 1]

        it_order = blk_up(0) + blk_up(1)
        for k in range(8):
            if k != 7:
                it_order += blk_gate(k)
            if k + 2 < 8:
                it_order += blk_up(k + 2)
            if k != 7:
                it_order += blk_down(k)
        attn_order = [SLOT_WZ, SLOT_WH, SLOT_WZ + 1, SLOT_WH + 1]
        chunk_order = it_order * 2 + attn_order + it_order * 2
        full_order = chunk_order * n_chunks

        def wslice(pos, kc, t):
            return wring[:, pos * 4096 + kc * 512 + t * 128:
                         pos * 4096 + kc * 512 + t * 128 + 128]

        def v3(ap, g):
            return ap.rearrange("p (g e) -> p g e", g=g)

        def vo1(ap):
            return ap.rearrange("p (g o) -> p g o", o=1)

        qg = {"V0": 0, "V1": 0}

        def stats_and_rows(x_sig, is_ln, xbuf, eps):
            qview = hb2[:, 0:4096]
            sq_sig = []
            cast_sig = []
            for fc in range(FC):
                xs = x_sig[fc] if isinstance(x_sig[fc], list) else [x_sig[fc]]
                B.em("scalar", lambda s, f=fc, xb=xbuf: s.activation(
                    qview[:, f * 512:(f + 1) * 512], xb[:, f * 512:(f + 1) * 512],
                    AF.Square, bias=0.0, scale=1.0),
                    waits=xs + [cap(T), cap(G), (V, qg["V0"])], incs=[(SS, 1)])
                sq_sig.append((SS, SS.n))
                if is_ln:
                    B.em("scalar", lambda s, f=fc, xb=xbuf: s.activation(
                        xnb[:, f * 512:(f + 1) * 512], xb[:, f * 512:(f + 1) * 512],
                        AF.Identity, bias=0.0, scale=1.0),
                        waits=xs + [cap(T)], incs=[(SS, 1)])
                    cast_sig.append((SS, SS.n))
            bsq, bsq_e = banks.acq()
            for kc in range(FC):
                B.em("tensor", lambda t, b=bsq, k=kc: t.matmul(
                    bankap(b, (slice(0, 1), slice(0, 512))), ones_bc[:, :],
                    qview[:, k * 512:(k + 1) * 512], start=(k == 0), stop=(k == 7)),
                    waits=(bsq_e if kc == 0 else []) + [sq_sig[kc]],
                    incs=([(T, 1)] if kc == 7 else []))
            if is_ln:
                bsum, bsum_e = banks.acq()
                for kc in range(FC):
                    B.em("tensor", lambda t, b=bsum, k=kc: t.matmul(
                        bankap(b, (slice(0, 1), slice(0, 512))), ones_bc[:, :],
                        xnb[:, k * 512:(k + 1) * 512], start=(k == 0), stop=(k == 7)),
                        waits=(bsum_e if kc == 0 else []) + [cast_sig[kc]],
                        incs=([(T, 1)] if kc == 7 else []))
                B.em("vector", lambda v, b=bsum: v.tensor_scalar_mul(
                    rowA[:, :], bankap(b, (slice(0, 1), slice(0, 512))), 1.0 / D),
                    waits=[cap(T)], incs=[(V, 1)])
                B.em("vector", lambda v: v.tensor_mul(
                    rowB[:, :], rowA[:, :], rowA[:, :]),
                    waits=[(V, V.n)], incs=[(V, 1)])
                B.em("vector", lambda v, b=bsq: v.scalar_tensor_tensor(
                    rowB[:, :], bankap(b, (slice(0, 1), slice(0, 512))), 1.0 / D,
                    rowB[:, :], ALU.mult, ALU.subtract),
                    waits=[(V, V.n)], incs=[(V, 1)])
                banks.release(bsum, cap(V))
                banks.release(bsq, cap(V))
                B.em("scalar", lambda s, e=eps: s.activation(
                    rowB[:, :], rowB[:, :], AF.Sqrt, bias=e, scale=1.0),
                    waits=[cap(V)], incs=[(SS, 1)])
                B.em("vector", lambda v: v.reciprocal(rowB[:, :], rowB[:, :]),
                     waits=[cap(SS)], incs=[(V, 1)])
                rstd_sig = (V, V.n)
                B.em("vector", lambda v: v.tensor_mul(
                    rowA[:, :], rowA[:, :], rowB[:, :]),
                    waits=[(V, V.n)], incs=[(V, 1)])
                rowA_sig = (V, V.n)
            else:
                B.em("scalar", lambda s, b=bsq, e=eps: s.activation(
                    rowB[:, :], bankap(b, (slice(0, 1), slice(0, 512))), AF.Sqrt,
                    bias=e, scale=1.0 / D),
                    waits=[cap(T)], incs=[(SS, 1)])
                banks.release(bsq, cap(SS))
                B.em("vector", lambda v: v.reciprocal(rowB[:, :], rowB[:, :]),
                     waits=[cap(SS)], incs=[(V, 1)])
                rstd_sig = (V, V.n)
                rowA_sig = None
            return rstd_sig, rowA_sig

        def bcast_to_sbuf(row_tile, dst, src_sig):
            bi, be = banks.acq()
            B.em("tensor", lambda t, b=bi, r=row_tile: t.matmul(
                bankap(b), ones_fr[:, :], r[:, :], start=True, stop=True),
                waits=be + [src_sig], incs=[(T, 1)])
            B.em("vector", lambda v, b=bi, d=dst: v.tensor_copy(d[:, :], bankap(b)),
                 waits=[cap(T), cap(G)], incs=[(V, 1)])
            banks.release(bi, cap(V))

        def layernorm(xbuf, x_sig, eps):
            rstd_sig, rowA_sig = stats_and_rows(x_sig, True, xbuf, eps)
            bcast_to_sbuf(rowB, Rs, rstd_sig)
            bcast_to_sbuf(rowA, MRs, rowA_sig)
            r_v = V.n
            xf_sig, xnb_sig = [], []
            for fc in range(FC):
                B.em("vector", lambda v, f=fc, xb=xbuf: v.tensor_tensor(
                    tnt[:, (f % 2) * 512:(f % 2 + 1) * 512],
                    xb[:, f * 512:(f + 1) * 512],
                    Rs[:, :], ALU.mult),
                    waits=[(V, r_v if fc < 2 else xf_sig[fc - 2][1]), cap(G)],
                    incs=[(V, 1)])
                tv = V.n
                B.em("vector", lambda v, f=fc: v.tensor_sub(
                    xf[:, f * 512:(f + 1) * 512],
                    tnt[:, (f % 2) * 512:(f % 2 + 1) * 512],
                    MRs[:, :]),
                    waits=[(V, tv)], incs=[(V, 1)])
                xf_sig.append((V, V.n))
                B.em("scalar", lambda s, f=fc: s.activation(
                    xnb[:, f * 512:(f + 1) * 512], xf[:, f * 512:(f + 1) * 512],
                    AF.Identity, bias=0.0, scale=1.0),
                    waits=[(V, V.n)], incs=[(SS, 1)])
                xnb_sig.append((SS, SS.n))
            return xf_sig, xnb_sig

        def rms(j, yac_sig):
            rstd_sig, _ = stats_and_rows(yac_sig, False, yac, RMS_EPS)
            bcast_to_sbuf(rowB, Rs, rstd_sig)
            r_v = V.n
            xf_sig, xnb_sig = [], []
            for fc in range(FC):
                B.em("vector", lambda v, f=fc, jj=j: v.tensor_single_scalar(
                    tnt[:, (f % 2) * 512:(f % 2 + 1) * 512],
                    yac[:, f * 512:(f + 1) * 512],
                    nw[:, jj * 8 + f:jj * 8 + f + 1], ALU.mult),
                    waits=[(V, r_v), cap(G)], incs=[(V, 1)])
                tv = V.n
                B.em("gpsimd", lambda g, f=fc: g.tensor_mul(
                    xf[:, f * 512:(f + 1) * 512],
                    tnt[:, (f % 2) * 512:(f % 2 + 1) * 512],
                    Rs[:, :]),
                    waits=[(V, tv)], incs=[(G, 1)])
                xf_sig.append((G, G.n))
                B.em("scalar", lambda s, f=fc: s.activation(
                    xnb[:, f * 512:(f + 1) * 512], xf[:, f * 512:(f + 1) * 512],
                    AF.Identity, bias=0.0, scale=1.0),
                    waits=[(G, G.n)], incs=[(SS, 1)])
                xnb_sig.append((SS, SS.n))
            return xf_sig, xnb_sig

        def router_mm(j, xf_sig):
            bi, be = banks.acq()
            for kc in range(FC):
                B.em("tensor", lambda t, b=bi, k=kc, jj=j: t.matmul(
                    bankap(b, (slice(0, 6), slice(0, 512))),
                    rw[:, jj * 48 + k * 6: jj * 48 + (k + 1) * 6],
                    xf[:, k * 512:(k + 1) * 512], start=(k == 0), stop=(k == 7)),
                    waits=(be if kc == 0 else []) + [xf_sig[kc]],
                    incs=([(T, 1)] if kc == 7 else []))
            B.em("vector", lambda v, b=bi, jj=j: v.tensor_scalar(
                rt_b6[:, :], bankap(b, (slice(0, 6), slice(0, 512))),
                rb[:, jj:jj + 1], gb[:, jj:jj + 1], ALU.add, ALU.add),
                waits=[cap(T)])
            B.em("vector", lambda v, b=bi, jj=j: v.tensor_single_scalar(
                rt_g6[:, :], bankap(b, (slice(0, 6), slice(0, 512))),
                rb[:, jj:jj + 1], ALU.add), incs=[(V, 1)])
            banks.release(bi, cap(V))
            return V.n

        def router_transposes(rt_v):
            bt1, bt1e = banks.acq()
            bt2, bt2e = banks.acq()
            for grp in range(4):
                B.em("tensor", lambda t, b=bt1, g_=grp: t.transpose(
                    bankap(b, (slice(0, 128), slice(g_ * 6, (g_ + 1) * 6))),
                    rt_b6[:, g_ * 128:(g_ + 1) * 128], idt[0:6, 0:6]),
                    waits=(bt1e if grp == 0 else []) + [(V, rt_v)])
            for grp in range(4):
                B.em("tensor", lambda t, b=bt2, g_=grp: t.transpose(
                    bankap(b, (slice(0, 128), slice(g_ * 6, (g_ + 1) * 6))),
                    rt_g6[:, g_ * 128:(g_ + 1) * 128], idt[0:6, 0:6]),
                    waits=(bt2e if grp == 0 else []),
                    incs=([(T, 1)] if grp == 3 else []))
            B.em("vector", lambda v, b=bt1: v.tensor_copy(
                rt_bt[:, :], bankap(b, (slice(0, 128), slice(0, 24)))),
                waits=[cap(T)])
            B.em("vector", lambda v, b=bt2: v.tensor_copy(
                rt_gt[:, :], bankap(b, (slice(0, 128), slice(0, 24)))),
                incs=[(V, 1)])
            banks.release(bt1, cap(V))
            banks.release(bt2, cap(V))
            B.em("vector", lambda v: v.drain())
            B.em("vector", lambda v: v.reduce_max(vo1(r_m1.ap()), v3(rt_bt.ap(), 4), AX))
            B.em("vector", lambda v: v.drain())
            for g_ in range(4):
                B.em("vector", lambda v, gg=g_: v.tensor_single_scalar(
                    r_eq[:, gg * 6:(gg + 1) * 6], rt_bt[:, gg * 6:(gg + 1) * 6],
                    r_m1[:, gg:gg + 1], ALU.is_ge))
            B.em("vector", lambda v: v.drain())
            for g_ in range(4):
                B.em("vector", lambda v, gg=g_: v.scalar_tensor_tensor(
                    r_b2[:, gg * 6:(gg + 1) * 6], r_eq[:, gg * 6:(gg + 1) * 6],
                    -30000.0, rt_bt[:, gg * 6:(gg + 1) * 6], ALU.mult, ALU.add))
            B.em("vector", lambda v: v.drain())
            B.em("vector", lambda v: v.reduce_max(vo1(r_m2.ap()), v3(r_b2.ap(), 4), AX))
            B.em("vector", lambda v: v.drain())
            for g_ in range(4):
                B.em("vector", lambda v, gg=g_: v.tensor_single_scalar(
                    r_sel[:, gg * 6:(gg + 1) * 6], rt_bt[:, gg * 6:(gg + 1) * 6],
                    r_m2[:, gg:gg + 1], ALU.is_ge))
            B.em("vector", lambda v: v.drain())
            B.em("vector", lambda v: v.tensor_mul(r_gs[:, :], rt_gt[:, :], r_sel[:, :]))
            B.em("vector", lambda v: v.drain())
            B.em("vector", lambda v: v.reduce_sum(vo1(r_S.ap()), v3(r_gs.ap(), 4), AX))
            B.em("vector", lambda v: v.drain())
            for g_ in range(4):
                B.em("vector", lambda v, gg=g_: v.tensor_scalar(
                    r_d[:, gg * 6:(gg + 1) * 6], rt_gt[:, gg * 6:(gg + 1) * 6],
                    2.0, r_S[:, gg:gg + 1], ALU.mult, ALU.subtract),
                    incs=([(V, 1)] if g_ == 3 else []))
            B.em("scalar", lambda s: s.activation(
                r_wp[:, :], r_d[:, :], AF.Sigmoid, bias=0.0, scale=1.0),
                waits=[cap(V)], incs=[(SS, 1)])
            B.em("vector", lambda v: v.tensor_mul(r_w[:, :], r_wp[:, :], r_sel[:, :]),
                 waits=[cap(SS)], incs=[(V, 1)])
            return V.n

        def router_wtrans(w_v):
            bw, bwe = banks.acq()
            for g_ in range(4):
                B.em("tensor", lambda t, b=bw, gg=g_: t.transpose(
                    bankap(b, (slice(0, 6), slice(gg * 128, (gg + 1) * 128))),
                    r_w[:, gg * 6:(gg + 1) * 6], idt[:, :]),
                    waits=(bwe if g_ == 0 else []) + [(V, w_v)],
                    incs=([(T, 1)] if g_ == 3 else []))
            B.em("vector", lambda v, b=bw: v.tensor_copy(
                w6b[:, :], bankap(b, (slice(0, 6), slice(0, 512)))),
                waits=[cap(T)], incs=[(V, 1)])
            banks.release(bw, cap(V))
            return V.n

        def phase_up(use0, use1, bias_tile, bias_off, xnb_sig, par):
            uses = [use0, use1]
            hoff = par * 4096
            for oc in range(FC):
                half = oc // 4
                i, pos, wl = uses[half]
                bi, be = banks.acq()
                for kc in range(FC):
                    incs = [(T, 1)] if kc == 7 else []
                    B.em("tensor", lambda t, b=bi, k=kc, p=pos, tt=oc % 4: t.matmul(
                        bankap(b), wslice(p, k, tt),
                        xnb[:, k * 512:(k + 1) * 512], start=(k == 0), stop=(k == 7)),
                        waits=(be if kc == 0 else []) +
                              ([wl] if kc == 0 else []) + [xnb_sig[kc]],
                        incs=incs)
                if oc % 4 == 3:
                    ring.freed(i, full_order)
                B.em("scalar", lambda s, b=bi, o=oc, bo=bias_off + oc, bt=bias_tile:
                     s.activation(hb2[:, hoff + o * 512: hoff + o * 512 + 512],
                                  bankap(b), AF.Identity,
                                  bias=bt[:, bo:bo + 1], scale=1.0),
                     waits=[cap(T), cap(G), (V, qg["V%d" % par])], incs=[(SS, 1)])
                banks.release(bi, cap(SS))
            return SS.n

        def phase_gate_prod(useg0, useg1, bias_off, wrow_bank, shared, h_ss, par,
                            free_ring=True, sum_into_prod=False):
            uses = [useg0, useg1]
            hoff = par * 4096
            for oc in range(FC):
                half = oc // 4
                i, pos, wl = uses[half]
                bi, be = banks.acq()
                for kc in range(FC):
                    incs = [(T, 1)] if kc == 7 else []
                    B.em("tensor", lambda t, b=bi, k=kc, p=pos, tt=oc % 4: t.matmul(
                        bankap(b), wslice(p, k, tt),
                        hb2[:, hoff + k * 512: hoff + (k + 1) * 512],
                        start=(k == 0), stop=(k == 7)),
                        waits=(be if kc == 0 else []) + ([wl] if kc == 0 else []) +
                              ([(SS, h_ss)] if kc == 0 else []),
                        incs=incs)
                if oc % 4 == 3 and free_ring:
                    ring.freed(i, full_order)
                sl = oc % 2
                B.em("scalar", lambda s, b=bi, ss=sl, bo=bias_off + oc,
                     bt=(bsg if shared else bgt_):
                     s.activation(sgt[:, ss * 512:(ss + 1) * 512], bankap(b), AF.Silu,
                                  bias=bt[:, bo:bo + 1], scale=1.0),
                     waits=[cap(T), cap(G)] + ([cap(V)] if shared else []),
                     incs=[(SS, 1)])
                banks.release(bi, cap(SS))
                if shared and sum_into_prod:
                    B.em("vector", lambda v, ss=sl, o=oc: v.tensor_tensor(
                        hb2[:, o * 512:(o + 1) * 512],
                        sgt[:, ss * 512:(ss + 1) * 512],
                        hb2[:, hoff + o * 512: hoff + (o + 1) * 512], ALU.mult),
                        waits=[cap(SS), cap(T), (V, qg["V0"])], incs=[(V, 1)])
                    B.em("gpsimd", lambda g, o=oc: g.tensor_tensor(
                        prod[:, o * 512:(o + 1) * 512],
                        prod[:, o * 512:(o + 1) * 512],
                        hb2[:, o * 512:(o + 1) * 512], ALU.add),
                        waits=[(V, V.n), cap(T)], incs=[(G, 1)])
                elif shared:
                    B.em("vector", lambda v, ss=sl, o=oc: v.tensor_tensor(
                        prod[:, o * 512:(o + 1) * 512],
                        sgt[:, ss * 512:(ss + 1) * 512],
                        hb2[:, hoff + o * 512: hoff + (o + 1) * 512], ALU.mult),
                        waits=[cap(SS), cap(T)], incs=[(V, 1)])
                else:
                    B.em("gpsimd", lambda g, ss=sl, o=oc: g.tensor_tensor(
                        p1t[:, ss * 512:(ss + 1) * 512],
                        sgt[:, ss * 512:(ss + 1) * 512],
                        hb2[:, hoff + o * 512: hoff + (o + 1) * 512], ALU.mult),
                        waits=[cap(SS), cap(V)], incs=[(G, 1)])
                    wb = bankap(wrow_bank).rearrange("p (o t) -> p o t", o=1)
                    B.em("vector", lambda v, ss=sl, o=oc, w=wb: v.tensor_tensor(
                        prod.ap()[:, o * 512:(o + 1) * 512].rearrange(
                            "p (o t) -> p o t", o=1),
                        p1t.ap()[:, ss * 512:(ss + 1) * 512].rearrange(
                            "p (o t) -> p o t", o=1),
                        w, ALU.mult),
                        waits=[cap(G), cap(T)], incs=[(V, 1)])
            if not shared:
                banks.release(wrow_bank, cap(V))
            return (G, G.n) if sum_into_prod else (V, V.n)

        def phase_down(used0, used1, first_slot, with_bdn, w6b_v, prod_sig, xf_sig,
                       yac_sig_out, free_ring=True):
            uses = [used0, used1]
            B.em("vector", lambda v: v.drain())
            for m in range(FC):
                half = m // 4
                i, pos, wl = uses[half]
                bi, be = banks.acq()
                for kc in range(FC):
                    incs = [(T, 1)] if (kc == 7 and not with_bdn) else []
                    B.em("tensor", lambda t, b=bi, k=kc, p=pos, tt=m % 4: t.matmul(
                        bankap(b), wslice(p, k, tt),
                        prod[:, k * 512:(k + 1) * 512],
                        start=(k == 0), stop=(k == 7 and not with_bdn)),
                        waits=(be if kc == 0 else []) + ([wl] if kc == 0 else []) +
                              ([prod_sig] if kc == 0 else []),
                        incs=incs)
                if with_bdn:
                    B.em("tensor", lambda t, b=bi, mm=m: t.matmul(
                        bankap(b), bdn[:, mm * 128:(mm + 1) * 128],
                        w6b[:, :], start=False, stop=True),
                        waits=[(V, w6b_v)], incs=[(T, 1)])
                if m % 4 == 3 and free_ring:
                    ring.freed(i, full_order)
                if first_slot:
                    B.em("vector", lambda v, b=bi, mm=m: v.scalar_tensor_tensor(
                        yac[:, mm * 512:(mm + 1) * 512], bankap(b),
                        csd[:, mm:mm + 1], xf[:, mm * 512:(mm + 1) * 512],
                        ALU.add, ALU.add),
                        waits=[cap(T), xf_sig[m]], incs=[(V, 1)])
                else:
                    B.em("vector", lambda v, b=bi, mm=m: v.tensor_tensor(
                        yac[:, mm * 512:(mm + 1) * 512], yac[:, mm * 512:(mm + 1) * 512],
                        bankap(b), ALU.add),
                        waits=[cap(T)], incs=[(V, 1)])
                if yac_sig_out is not None:
                    yac_sig_out[m] = (V, V.n)
                banks.release(bi, cap(V))

        def ffn_iter(j, xf_sig, xnb_sig, fillers=None):
            rt_v = router_mm(j, xf_sig)
            u0 = [ring.use(full_order, s) for s in blk_up(0)]
            h_ss = {0: phase_up(u0[0], u0[1], bup, 0, xnb_sig, 0)}
            w_v = router_transposes(rt_v)
            u1 = [ring.use(full_order, s) for s in blk_up(1)]
            h_ss[1] = phase_up(u1[0], u1[1], bup, 8, xnb_sig, 1)
            w6b_v = router_wtrans(w_v)
            yac_sig = [None] * FC
            for k in range(8):
                if fillers and k in (1, 3, 5, 7):
                    fillers.pop(0)()
                shared = k >= E
                if not shared:
                    bw, bwe = banks.acq()
                    B.em("tensor", lambda t, b=bw, ee=k: t.matmul(
                        bankap(b), oh6[:, ee * 128:(ee + 1) * 128], w6b[0:6, :],
                        start=True, stop=True),
                        waits=bwe + [(V, w6b_v)], incs=[(T, 1)])
                else:
                    bw = None
                if k == 7:
                    g_ = hold_sg
                else:
                    g_ = [ring.use(full_order, s) for s in blk_gate(k)]
                if k == 6:
                    hold_sg = g_
                prod_v = phase_gate_prod(
                    g_[0], g_[1], (k * 8 if not shared else 0), bw, shared,
                    h_ss[k], k % 2, free_ring=(k != 6),
                    sum_into_prod=(k == 7))
                if shared:
                    qg["V0"] = V.n
                    if k == 7:
                        qg["V1"] = V.n
                if k + 2 < 8:
                    nu = [ring.use(full_order, s) for s in blk_up(k + 2)]
                if k == 7:
                    d_ = hold_sd
                else:
                    d_ = [ring.use(full_order, s) for s in blk_down(k)]
                if k == 6:
                    hold_sd = d_
                if k + 2 < 8:
                    kk = k + 2
                    bias_t, bias_o = (bup, kk * 8) if kk < E else (bsup, (kk - E) * 8)
                    h_ss[kk] = phase_up(nu[0], nu[1], bias_t, bias_o, xnb_sig, kk % 2)
                if k != 6:
                    phase_down(d_[0], d_[1], k == 0, k == 7, w6b_v, prod_v,
                               xf_sig, yac_sig, free_ring=True)
            return rms(j, yac_sig)

        def at(par, idx):
            off = par * 2048 + idx * 512
            return yac.ap()[:, off:off + 512]

        atrk = {"scan_v": [0, 0], "b_g": [0, 0], "carry_ss": [0] * FC}

        def attn(xsrc, xdst, xnb_sig):
            uz = [ring.use(full_order, SLOT_WZ)]
            uh = [ring.use(full_order, SLOT_WH)]
            uz.append(ring.use(full_order, SLOT_WZ + 1))
            uh.append(ring.use(full_order, SLOT_WH + 1))
            x2_sig = []
            for fc in range(FC):
                par = fc % 2
                half = fc // 4
                iz, pz, wlz = uz[half]
                ih, ph, wlh = uh[half]
                bk, bke = banks.acq()
                bt, bte = banks.acq()
                for kc in range(FC):
                    incs = [(T, 1)] if kc == 7 else []
                    B.em("tensor", lambda t, b=bk, k=kc, p=pz, tt=fc % 4: t.matmul(
                        bankap(b), wslice(p, k, tt),
                        xnb[:, k * 512:(k + 1) * 512], start=(k == 0), stop=(k == 7)),
                        waits=(bke if kc == 0 else []) + ([wlz] if kc == 0 else []) +
                              [xnb_sig[kc]],
                        incs=incs)
                if fc % 4 == 3:
                    ring.freed(iz, full_order)
                for kc in range(FC):
                    incs = [(T, 1)] if kc == 7 else []
                    B.em("tensor", lambda t, b=bt, k=kc, p=ph, tt=fc % 4: t.matmul(
                        bankap(b), wslice(p, k, tt),
                        xnb[:, k * 512:(k + 1) * 512], start=(k == 0), stop=(k == 7)),
                        waits=(bte if kc == 0 else []) + ([wlh] if kc == 0 else []),
                        incs=incs)
                if fc % 4 == 3:
                    ring.freed(ih, full_order)
                kp = T.n
                pw = [(T, kp), (V, atrk["scan_v"][par]), (G, atrk["b_g"][par])]
                B.em("scalar", lambda s, b=bk, f=fc, p=par: s.activation(
                    at(p, 0), bankap(b), AF.Sigmoid, bias=nbz[:, f:f + 1], scale=-1.0),
                    waits=pw, incs=[(SS, 1)])
                a_ss = SS.n
                B.em("scalar", lambda s, b=bt, f=fc, p=par: s.activation(
                    at(p, 2), bankap(b), AF.Sigmoid, bias=bh[:, f:f + 1], scale=1.0))
                B.em("scalar", lambda s, b=bt, f=fc, p=par: s.activation(
                    at(p, 3), bankap(b), AF.Relu, bias=bh[:, f:f + 1], scale=1.0),
                    incs=[(SS, 1)])
                r_ss = SS.n
                banks.release(bk, (SS, a_ss))
                banks.release(bt, (SS, r_ss))
                B.em("vector", lambda v, p=par: v.tensor_scalar(
                    at(p, 1), at(p, 0), -1.0, 1.0, ALU.mult, ALU.add),
                    waits=[(SS, a_ss), (G, atrk["b_g"][par])])
                B.em("vector", lambda v, p=par: v.scalar_tensor_tensor(
                    at(p, 2), at(p, 2), 0.5, at(p, 3), ALU.min, ALU.add),
                    waits=[(SS, r_ss)], incs=[(V, 1)])
                gt_v = V.n
                B.em("gpsimd", lambda g, p=par: g.tensor_mul(
                    at(p, 3), at(p, 1), at(p, 2)),
                    waits=[(V, gt_v)], incs=[(G, 1)])
                atrk["b_g"][par] = G.n
                b_g = G.n
                B.em("vector", lambda v: v.drain())
                for s_ in range(SEQ):
                    B.em("vector", lambda v, f=fc, ss=s_, p=par: v.tensor_tensor_scan(
                        xf[:, f * 512 + ss * 128: f * 512 + (ss + 1) * 128],
                        at(p, 0)[:, ss * 128:(ss + 1) * 128],
                        at(p, 3)[:, ss * 128:(ss + 1) * 128],
                        carry[:, f * 4 + ss: f * 4 + ss + 1],
                        ALU.mult, ALU.add),
                        waits=([(G, b_g), (SS, atrk["carry_ss"][fc])]
                               if s_ == 0 else []),
                        incs=([(V, 1)] if s_ == SEQ - 1 else []))
                atrk["scan_v"][par] = V.n
                sc_v = V.n
                B.em("scalar", lambda s, f=fc: s.activation(
                    carry[:, f * 4: f * 4 + 4],
                    xf.ap()[:, f * 512:(f + 1) * 512].rearrange(
                        "p (s t) -> p s t", s=4)[:, :, 127],
                    AF.Identity, bias=0.0, scale=1.0),
                    waits=[(V, sc_v)], incs=[(SS, 1)])
                atrk["carry_ss"][fc] = SS.n
                B.em("gpsimd", lambda g, f=fc, xs=xsrc, xd=xdst: g.tensor_add(
                    xd[:, f * 512:(f + 1) * 512], xs[:, f * 512:(f + 1) * 512],
                    xf[:, f * 512:(f + 1) * 512]),
                    waits=[(V, sc_v)], incs=[(G, 1)])
                x2_sig.append((G, G.n))
            return x2_sig

        state = {"in_tg": 0}

        def emit_in_dmas(ci):
            t0 = ci * TLOC
            for pp_ in range(2):
                B.em("gpsimd", lambda g, tt=t0, pp=pp_: g.dma_start(
                    out=stage.ap()[:, pp * 2048:(pp + 1) * 2048].rearrange(
                        "p (s d) -> p s d", s=2),
                    in_=x_in[2 * pp:2 * pp + 2, tt:tt + TLOC, :].rearrange(
                        "s t d -> t s d")),
                    waits=[(T, state["in_tg"])], incs=[(XL, 16)])
            return XL.n

        def emit_in_passes(xl_v, x3_guard):
            x0_sig = [None] * FC
            for dc in range(FC):
                vn = None
                for s_ in range(SEQ):
                    bi, be = banks.acq()
                    B.em("tensor", lambda t, b=bi, ss=s_, d=dc: t.transpose(
                        bankap(b, (slice(0, 128), slice(0, 128))),
                        stage[:, ss * 1024 + d * 128: ss * 1024 + (d + 1) * 128],
                        idt[:, :]),
                        waits=be + [(XL, xl_v)],
                        incs=[(T, 1)])
                    guard = [x3_guard[dc]] if x3_guard is not None else []
                    if s_ % 2 == 0:
                        B.em("vector", lambda v, b=bi, ss=s_, d=dc: v.tensor_copy(
                            xA[:, d * 512 + ss * 128: d * 512 + (ss + 1) * 128],
                            bankap(b, (slice(0, 128), slice(0, 128)))),
                            waits=[cap(T)] + guard, incs=[(V, 1)])
                        vn = (V, V.n)
                        banks.release(bi, cap(V))
                    else:
                        B.em("scalar", lambda s, b=bi, ss=s_, d=dc: s.activation(
                            xA[:, d * 512 + ss * 128: d * 512 + (ss + 1) * 128],
                            bankap(b, (slice(0, 128), slice(0, 128))),
                            AF.Identity, bias=0.0, scale=1.0),
                            waits=[cap(T)] + guard, incs=[(SS, 1)])
                        banks.release(bi, cap(SS))
                x0_sig[dc] = [vn, (SS, SS.n)]
            state["in_tg"] = T.n
            return x0_sig

        def emit_out_group(ci_prev, tt, ph, x3s):
            ro = 4096 + (ph % 2) * 1024
            ogs = OGs[ph % 2]
            og_need = 16 * (2 * ci_prev + (ph // 2))
            for dc in range(FC):
                bi, be = banks.acq()
                B.em("tensor", lambda t, b=bi, ss=ph, d=dc: t.transpose(
                    bankap(b, (slice(0, 128), slice(0, 128))),
                    xB[:, d * 512 + ss * 128: d * 512 + (ss + 1) * 128],
                    idt[:, :]),
                    waits=be + [x3s[dc]],
                    incs=[(T, 1)])
                if dc % 2 == 0:
                    B.em("vector", lambda v, b=bi, d=dc: v.tensor_copy(
                        stage[:, ro + d * 128: ro + (d + 1) * 128],
                        bankap(b, (slice(0, 128), slice(0, 128)))),
                        waits=[cap(T), (ogs, og_need)],
                        incs=[(V, 1)])
                    banks.release(bi, cap(V))
                else:
                    B.em("scalar", lambda s, b=bi, d=dc: s.activation(
                        stage[:, ro + d * 128: ro + (d + 1) * 128],
                        bankap(b, (slice(0, 128), slice(0, 128))),
                        AF.Identity, bias=0.0, scale=1.0),
                        waits=[cap(T), (ogs, og_need)],
                        incs=[(SS, 1)])
                    banks.release(bi, cap(SS))
            B.em("gpsimd", lambda g, tt=tt, pp=ph, r=ro: g.dma_start(
                out=y_out[pp:pp + 1, tt:tt + TLOC, :].rearrange(
                    "s t d -> t s d"),
                in_=stage.ap()[:, r:r + 1024].rearrange("p (s d) -> p s d", s=1)),
                waits=[(V, V.n), (SS, SS.n)], incs=[(ogs, 16)])

        pending_in = None
        x3_prev = None
        prev_out = None
        for ci in range(n_chunks):
            t0 = ci * TLOC
            if pending_in is None:
                B.em("vector", lambda v: v.drain())
                xl_v = emit_in_dmas(ci)
                x0_sig = emit_in_passes(xl_v, None)
            else:
                x0_sig = emit_in_passes(pending_in, x3_prev)
            if prev_out is not None:
                p_ci, p_t0, p_x3 = prev_out
                for ph in range(4):
                    emit_out_group(p_ci, p_t0, ph, p_x3)
            xf_sig, xnb_sig = layernorm(xA, x0_sig, LN_EPS)
            xf_sig, xnb_sig = ffn_iter(0, xf_sig, xnb_sig)
            xf_sig, xnb_sig = ffn_iter(1, xf_sig, xnb_sig)
            B.em("vector", lambda v: v.drain())
            x1_sig = []
            for fc in range(FC):
                B.em("vector", lambda v, f=fc: v.scalar_tensor_tensor(
                    xB[:, f * 512:(f + 1) * 512], xf[:, f * 512:(f + 1) * 512],
                    0.5, xA[:, f * 512:(f + 1) * 512], ALU.mult, ALU.add),
                    waits=[xf_sig[fc], cap(SS)], incs=[(V, 1)])
                x1_sig.append((V, V.n))
            xf_sig, xnb_sig = layernorm(xB, x1_sig, LN_EPS)
            x2_sig = attn(xB, xA, xnb_sig)
            xf_sig, xnb_sig = layernorm(xA, x2_sig, LN_EPS)
            xf_sig, xnb_sig = ffn_iter(0, xf_sig, xnb_sig)
            xf_sig, xnb_sig = ffn_iter(1, xf_sig, xnb_sig)
            B.em("vector", lambda v: v.drain())
            x3_sig = []
            for fc in range(FC):
                B.em("vector", lambda v, f=fc: v.scalar_tensor_tensor(
                    xB[:, f * 512:(f + 1) * 512], xf[:, f * 512:(f + 1) * 512],
                    0.5, xA[:, f * 512:(f + 1) * 512], ALU.mult, ALU.add),
                    waits=[xf_sig[fc], cap(SS)], incs=[(V, 1)])
                x3_sig.append((V, V.n))
            if ci + 1 < n_chunks:
                pending_in = emit_in_dmas(ci + 1)
                x3_prev = x3_sig
            prev_out = (ci, t0, x3_sig)

        p_ci, p_t0, p_x3 = prev_out
        for ph in range(4):
            emit_out_group(p_ci, p_t0, ph, p_x3)

        B.em("gpsimd", lambda g: g.nop(),
             waits=[(OGs[0], 32 * n_chunks), (OGs[1], 32 * n_chunks)])

        assert ring.use_i == len(full_order), (ring.use_i, len(full_order))

        with nc.Block() as block:
            B.replay(block)
    return nc


def kernel(**inputs):
    nc = build_nc(NCHUNK_FULL)
    in_maps = [host_inputs(inputs, c) for c in range(8)]
    res = run_bass_kernel_spmd(nc, in_maps, core_ids=list(range(8)))
    out = np.concatenate([r["y_out"] for r in res.results], axis=0)
    return out.astype(np.float32)
